# version 07 (final)
# baseline (speedup 1.0000x reference)
import math
import numpy as np
import ml_dtypes

import concourse.bass as bass
import concourse.mybir as mybir
from concourse.bass_utils import run_bass_kernel_spmd

F32 = mybir.dt.float32
F16 = mybir.dt.float16
BF16 = mybir.dt.bfloat16
AF = mybir.ActivationFunctionType
ALU = mybir.AluOpType
AX = mybir.AxisListType

NH = 32
Q_HEAD = 192
D, S, B_, T, HD = 4096, 2048, 4, 1024, 128
MASKVAL = -1992.0

_UPTO = 4

BC_QC = 0
BC_QUP = 12
BC_KVN = 60
BC_KVR = 64
BC_KVUP = 65
BC_BO = 129
BC_QKV = 161
BC_OUT = 257
BC_N = 289
BS_QUP = 0
BS_QKVQ = 48
BS_N = 80


def _bf(x):
    return np.ascontiguousarray(np.asarray(x).astype(ml_dtypes.bfloat16))


def _blocked(wt, ic, mt, f=128):
    return np.ascontiguousarray(wt.reshape(ic, 128, mt, f).transpose(2, 1, 0, 3))


class Ctr:
    def __init__(self, stack, nc, name):
        self.sem = stack.enter_context(nc.semaphore(name))
        self.n = 0


class Bank:
    def __init__(self):
        self.marks = []

    def wr(self):
        m, self.marks = self.marks, []
        return m

    def rd(self, ctr):
        self.marks.append((ctr, ctr.n))


class Prog:

    def __init__(self):
        self.ops = {k: [] for k in ("pe", "act", "dve", "pool", "sp")}
        self.waited = {k: {} for k in self.ops}
        self.self_ctr = {}

    def op(self, eng, build, waits=(), inc=None, amt=1):
        sc = self.self_ctr.get(eng)
        if sc is not None:
            waits = list(waits) + [(sc, sc.n)]
            if inc is None:
                inc, amt = sc, 1
        if inc is False:
            inc = None
        wl = []
        for (c, v) in waits:
            if v <= 0:
                continue
            key = id(c)
            if self.waited[eng].get(key, -1) >= v:
                continue
            self.waited[eng][key] = v
            wl.append((c.sem, v))
        sem = None
        if inc is not None:
            inc.n += amt
            sem = inc.sem

        def fn(e, _wl=wl, _b=build, _sem=sem, _amt=amt):
            for (s, v) in _wl:
                e.wait_ge(s, v)
            ins = _b(e)
            if _sem is not None:
                ins.then_inc(_sem, _amt)
        self.ops[eng].append(fn)

    def wait(self, eng, waits):
        self.op(eng, lambda e: e.nop(), waits=waits, inc=False)


def build_nc(upto=None):
    if upto is None:
        upto = _UPTO
    nc = bass.Bass("TRN2", target_bir_lowering=False)
    dt = nc.dram_tensor

    x0 = dt("x0", [T, D], F32, kind="ExternalInput").ap()
    wqcB = dt("wqcB", [12 * 128, 32 * 128], F32, kind="ExternalInput").ap()
    wqupB = dt("wqupB", [48 * 128, 12 * 128], F32, kind="ExternalInput").ap()
    wkvcnB = dt("wkvcnB", [4 * 128, 32 * 128], F32, kind="ExternalInput").ap()
    wkvcrB = dt("wkvcrB", [128, 32 * 64], F32, kind="ExternalInput").ap()
    wkvupB = dt("wkvupB", [64 * 128, 512], F32, kind="ExternalInput").ap()
    woB = dt("woB", [128, D], F32, kind="ExternalInput").ap()
    wqkvB = dt("wqkvB", [96 * 128, 32 * 128], F32, kind="ExternalInput").ap()
    woutB = dt("woutB", [32 * 128, 32 * 128], F32, kind="ExternalInput").ap()
    wffnB = dt("wffnB", [32 * 128, 32 * 128], F16, kind="ExternalInput").ap()
    srepB = dt("srepB", [32 * 128, 32 * 128], F16, kind="ExternalInput").ap()
    biasD = dt("biasD", [128, BC_N], F32, kind="ExternalInput").ap()
    identB = dt("identB", [128, 128], BF16, kind="ExternalInput").ap()
    identH = dt("identH", [128, 128], F16, kind="ExternalInput").ap()
    maskD = dt("maskD", [128, 16384], BF16, kind="ExternalInput").ap()
    kidxD = dt("kidxD", [128, 16], F32, kind="ExternalInput").ap()
    qglobD = dt("qglobD", [1, T], F32, kind="ExternalInput").ap()

    yout = dt("yout", [T, D], F32, kind="ExternalOutput").ap()

    xA = dt("xA", [T, D], F32).ap()
    xB = dt("xB", [T, D], F32).ap()
    q_dram = dt("q_dram", [6144, T], BF16).ap()
    cc_in_a = dt("cc_in_a", [576, T], BF16)
    cc_out_a = dt("cc_out_a", [2 * 576, T], BF16)
    qb_dram = dt("qb_dram", [4096, T], BF16).ap()
    cc_in_b = dt("cc_in_b", [8192, T], BF16)
    cc_out_b = dt("cc_out_b", [2 * 8192, T], BF16)

    from contextlib import ExitStack
    with ExitStack() as st:
        Ec = st.enter_context

        A_xhat = Ec(nc.sbuf_tensor("A_xhat", [128, 32 * 1024], BF16))
        A_y = Ec(nc.sbuf_tensor("A_y", [128, 32 * 1024], F16))
        A_st = Ec(nc.sbuf_tensor("A_st", [128, 8192], F32))
        A_sb = Ec(nc.sbuf_tensor("A_sb", [128, 8192], BF16))
        A_yb = A_y.bitcast(BF16)
        A_stH = A_st.bitcast(F16)
        A_stB = A_st.bitcast(BF16)
        A_sbF = A_sb.bitcast(F32)

        ident = Ec(nc.sbuf_tensor("ident", [128, 128], BF16))
        identh = Ec(nc.sbuf_tensor("identh", [128, 128], F16))
        ones = Ec(nc.sbuf_tensor("ones", [128, 512], BF16))
        negones = Ec(nc.sbuf_tensor("negones", [1, 128], BF16))
        bias_cols = Ec(nc.sbuf_tensor("bias_cols", [128, BC_N], F32))
        bias_scl = Ec(nc.sbuf_tensor("bias_scl", [128, BS_N], F32))
        colbuf = Ec(nc.sbuf_tensor("colbuf", [128, 1], BF16))
        rowbuf = Ec(nc.sbuf_tensor("rowbuf", [1, 128], BF16))
        kvrope = Ec(nc.sbuf_tensor("kvrope", [64, 2048], BF16))
        qh_n = [Ec(nc.sbuf_tensor(f"qh_n{i}", [128, 1024], BF16)) for i in range(2)]
        qh_r = [Ec(nc.sbuf_tensor(f"qh_r{i}", [64, 1024], BF16)) for i in range(2)]
        qkv_tmp = [Ec(nc.sbuf_tensor(f"qkv_tmp{i}", [128, 1024], BF16))
                   for i in range(2)]
        ctxa = Ec(nc.sbuf_tensor("ctxa", [128, 1024], BF16))
        bcast = Ec(nc.sbuf_tensor("bcast", [128, 1024], BF16))
        Bg = Ec(nc.sbuf_tensor("Bg", [1, 2048], BF16))
        stats = Ec(nc.sbuf_tensor("stats", [128, 8], F32))
        epsc = Ec(nc.sbuf_tensor("epsc", [128, 2], F32))
        acol = Ec(nc.sbuf_tensor("acol", [128, 32], F32))
        kidx = Ec(nc.sbuf_tensor("kidx", [128, 16], F32))
        qglob = Ec(nc.sbuf_tensor("qglob", [1, T], F32))

        xtv = [A_st.ap()[:, 0:4096], A_st.ap()[:, 4096:8192]]
        xhat_tmp = A_sb.ap()[:, 4096:8192]
        Ebuf = A_yb.ap()[:, 0:4096]
        kvnope = A_yb.ap()[:, 16384:24576]
        maskv = A_yb.ap()[:, 16384:32768]
        qc_sb = A_yb.ap()[:, 0:12288]
        ctxacc = A_y.bitcast(F32).ap()[:, 2048:3072]
        kh_a = [A_yb.ap()[:, 24576 + i * 2048:24576 + (i + 1) * 2048]
                for i in range(2)]
        vh_a = [A_yb.ap()[:, 28672 + i * 2048:28672 + (i + 1) * 2048]
                for i in range(2)]
        kh_b = [A_sb.ap()[:, i * 2048:(i + 1) * 2048] for i in range(2)]
        vhf_b = [A_yb.ap()[:, 4096 + i * 2048:4096 + (i + 1) * 2048]
                 for i in range(2)]
        vh_b = [A_yb.ap()[:, 8192 + i * 2048:8192 + (i + 1) * 2048]
                for i in range(2)]
        kvs_ld = [A_sbF.ap()[:, i * 512:(i + 1) * 512] for i in range(2)]
        kvs_bf = [A_sb.ap()[:, 2048 + i * 512:2048 + (i + 1) * 512]
                  for i in range(2)]
        woT_sb = A_sb.ap()[:, 4096:8192]
        normrow = A_st.ap()[0:1, 0:3072]
        Bf = A_st.ap()[0:1, 3072:4096]
        rsum = A_stB.ap()[0:1, 8192:9216]
        krope_nrm = A_st.ap()[0:1, 5120:7168]
        sqbuf = A_stB.ap()[:, 14336:16384]

        psA = Ec(nc.psum_tensor("psA", [128, 1024], F32))
        psB = Ec(nc.psum_tensor("psB", [128, 1024], F32))
        psC = Ec(nc.psum_tensor("psC", [128, 1024], F32))
        psD = Ec(nc.psum_tensor("psD", [128, 1024], F32))
        bkA, bkB, bkC = Bank(), Bank(), Bank()
        bkDh = [Bank(), Bank()]

        Lw = [Ctr(st, nc, f"Lw{i}") for i in range(2)]
        Lx = [Ctr(st, nc, f"Lx{i}") for i in range(2)]
        Lh = [Ctr(st, nc, f"Lh{i}") for i in range(2)]
        Lonce = Ctr(st, nc, "Lonce")
        Lcc = Ctr(st, nc, "Lcc")
        Lq = Ctr(st, nc, "Lq")
        Lqb = Ctr(st, nc, "Lqb")
        Lsx = [Ctr(st, nc, f"Lsx{i}") for i in range(2)]
        CCa = Ctr(st, nc, "CCa")
        CCb = Ctr(st, nc, "CCb")
        PE = Ctr(st, nc, "PE")
        ACT = Ctr(st, nc, "ACT")
        DVE = Ctr(st, nc, "DVE")
        POOL = Ctr(st, nc, "POOL")
        STO = [Ctr(st, nc, f"STO{i}") for i in range(2)]

        p = Prog()
        p.self_ctr = {"dve": DVE, "act": ACT, "pool": POOL}

        def barrier():
            snap = [(PE, PE.n), (ACT, ACT.n), (DVE, DVE.n), (POOL, POOL.n),
                    (Lw[0], Lw[0].n), (Lw[1], Lw[1].n), (Lx[0], Lx[0].n),
                    (Lx[1], Lx[1].n), (Lh[0], Lh[0].n), (Lh[1], Lh[1].n),
                    (Lonce, Lonce.n), (Lcc, Lcc.n), (Lq, Lq.n),
                    (Lqb, Lqb.n), (Lsx[0], Lsx[0].n), (Lsx[1], Lsx[1].n)]
            for eng in p.ops:
                p.wait(eng, snap)

        def bcol(j, n=128):
            return bias_cols[0:n, j:j + 1]

        def _emit():
            p.op("sp", lambda e: e.dma_start(out=ident[:], in_=identB), inc=Lonce, amt=16)
            p.op("sp", lambda e: e.dma_start(out=identh[:], in_=identH), inc=Lonce, amt=16)
            p.op("sp", lambda e: e.dma_start(out=bias_cols[:], in_=biasD), inc=Lonce, amt=16)
            p.op("sp", lambda e: e.dma_start(out=kidx[:], in_=kidxD), inc=Lonce, amt=16)
            p.op("sp", lambda e: e.dma_start(out=qglob[:], in_=qglobD), inc=Lonce, amt=16)
            ld0 = Lonce.n
            p.op("dve", lambda e: e.memset(ones[:], 1.0), inc=DVE)
            p.op("dve", lambda e: e.memset(epsc[:, 0:1], 1e-5), inc=DVE)
            p.op("dve", lambda e: e.memset(epsc[:, 1:2], 1e-30), inc=DVE)
            p.op("dve", lambda e: e.memset(negones[:], -1.0), inc=DVE)
            p.op("dve", lambda e: e.tensor_scalar(
                bias_scl[:, BS_QUP:BS_QUP + 48], bias_cols[:, BC_QUP:BC_QUP + 48],
                1.0 / math.sqrt(Q_HEAD), None, ALU.mult), waits=[(Lonce, ld0)], inc=DVE)
            p.op("dve", lambda e: e.tensor_scalar(
                bias_scl[:, BS_QKVQ:BS_QKVQ + 32], bias_cols[:, BC_QKV:BC_QKV + 32],
                HD ** -0.5, None, ALU.mult), inc=DVE)

            def fused_pass(src_x, dst_x, psrc_gen, do_ln, skip_store=False):
                xt_dve = [0, 0]
                xst = [None, None]
                for tt in range(8):
                    xb = xtv[tt % 2]
                    stc = STO[tt % 2] if dst_x is yout else Lsx[tt % 2]
                    p.op("sp", lambda e, _xb=xb, _tt=tt: e.dma_start(
                        out=_xb, in_=src_x[_tt * 128:(_tt + 1) * 128, :]),
                        waits=[(DVE, xt_dve[tt % 2]), (ACT, ACT.n),
                               (stc, stc.n)],
                        inc=Lx[tt % 2], amt=16)
                    ldv = (Lx[tt % 2], Lx[tt % 2].n)
                    if psrc_gen is not None:
                        for ft in range(8):
                            ps, pev, bank = psrc_gen(tt, ft)
                            p.op("dve",
                                 lambda e, _xb=xb, _ft=ft, _ps=ps: e.tensor_tensor(
                                     _xb[:, _ft * 512:(_ft + 1) * 512],
                                     _xb[:, _ft * 512:(_ft + 1) * 512],
                                     _ps, ALU.add),
                                 waits=[ldv, (PE, pev)], inc=DVE)
                            bank.rd(DVE)
                    else:
                        p.wait("dve", [ldv])
                    if do_ln:
                        p.op("dve", lambda e, _xb=xb: e.scalar_tensor_tensor(
                            xhat_tmp, _xb, 1.0, _xb, ALU.mult, ALU.mult,
                            accum_out=stats[:, 1:2]), waits=[(PE, PE.n)], inc=DVE)
                        p.op("dve", lambda e, _xb=xb: e.reduce_sum(
                            stats[:, 0:1], _xb, axis=AX.X), inc=DVE)
                        p.op("dve", lambda e: e.tensor_scalar(
                            stats[:, 2:3], stats[:, 0:1], 1.0 / D, None, ALU.mult),
                            inc=DVE)
                        p.op("dve", lambda e: e.tensor_tensor(
                            stats[:, 6:7], stats[:, 2:3], stats[:, 2:3], ALU.mult),
                            inc=DVE)
                        p.op("dve", lambda e: e.scalar_tensor_tensor(
                            stats[:, 5:6], stats[:, 1:2], 1.0 / D, stats[:, 6:7],
                            ALU.mult, ALU.subtract), inc=DVE)
                        dvn = DVE.n
                        p.op("act", lambda e: e.activation(
                            stats[:, 6:7], stats[:, 5:6], AF.Ln,
                            bias=epsc[:, 0:1], scale=1.0),
                            waits=[(DVE, dvn)], inc=ACT)
                        p.op("act", lambda e: e.activation(
                            stats[:, 3:4], stats[:, 6:7], AF.Exp, bias=0.0, scale=-0.5),
                            inc=ACT)
                        acn = ACT.n
                        p.op("dve", lambda e: e.scalar_tensor_tensor(
                            stats[:, 4:5], stats[:, 2:3], -1.0, stats[:, 3:4],
                            ALU.mult, ALU.mult), waits=[(ACT, acn)], inc=DVE)
                        p.op("dve", lambda e, _xb=xb: e.tensor_scalar(
                            xhat_tmp, _xb, stats[:, 3:4], stats[:, 4:5],
                            ALU.mult, ALU.add), inc=DVE)
                        xhd = DVE.n
                        xv = A_xhat.ap().rearrange("p (f c) -> p f c", c=1024)
                        for g in range(8):
                            hf = g % 2
                            bank = bkDh[hf]
                            wv = bank.wr() + [(DVE, xhd)]
                            for j in range(4):
                                fc = g * 4 + j
                                p.op("pe", lambda e, _hf=hf, _j=j, _fc=fc: e.matmul(
                                    psD[:, _hf * 512 + _j * 128:
                                        _hf * 512 + (_j + 1) * 128],
                                    xhat_tmp[:, _fc * 128:(_fc + 1) * 128],
                                    ident[:], start=True, stop=True),
                                    waits=wv, inc=(PE if j == 3 else None))
                                wv = ()
                            pen = PE.n
                            p.op("act", lambda e, _hf=hf, _g=g, _tt=tt: e.copy(
                                xv[:, _g * 4:(_g + 1) * 4, _tt * 128:(_tt + 1) * 128],
                                psD.ap()[:, _hf * 512:(_hf + 1) * 512].rearrange(
                                    "p (a b) -> p a b", a=4)),
                                waits=[(PE, pen)], inc=ACT)
                            bank.rd(ACT)
                    if not skip_store:
                        wvs = [(DVE, DVE.n), (ACT, ACT.n), ldv]
                        p.op("sp", lambda e, _xb=xb, _tt=tt: e.dma_start(
                            out=dst_x[_tt * 128:(_tt + 1) * 128, :], in_=_xb),
                            waits=wvs, inc=stc, amt=16)
                    xt_dve[tt % 2] = DVE.n

            def stream_mm_fm(wblk, n_mt, n_ic, out_cb, fdim=128, rhs=None,
                             mt_order=None, after_mt=None):
                wv_ = wblk.rearrange("(m p) c -> m p c", p=128)
                prev_pool = [0, 0]
                prev_pe = [0, 0]
                for mti, mt in enumerate(mt_order if mt_order is not None
                                         else range(n_mt)):
                    sl = mt % 2
                    lb = A_st.ap()[:, sl * 4096:sl * 4096 + n_ic * fdim]
                    bb = A_sb.ap()[:, sl * 4096:sl * 4096 + n_ic * fdim]
                    p.op("sp", lambda e, _mt=mt, _lb=lb: e.dma_start(
                        out=_lb, in_=wv_[_mt]),
                        waits=[(POOL, prev_pool[sl])], inc=Lw[sl], amt=16)
                    ldn = Lw[sl].n
                    p.op("pool", lambda e, _lb=lb, _bb=bb: e.tensor_copy(_bb, _lb),
                         waits=[(Lw[sl], ldn), (PE, prev_pe[sl])], inc=POOL)
                    prev_pool[sl] = POOL.n
                    cn = POOL.n
                    ps, bank = (psA, bkA) if sl == 0 else (psB, bkB)
                    wv = bank.wr() + [(POOL, cn)]
                    for nh in range(2):
                        for ic in range(n_ic):
                            src = (rhs if rhs is not None else A_xhat.ap())
                            p.op("pe", lambda e, _ps=ps, _nh=nh, _bb=bb, _ic=ic,
                                 _f=(ic == 0), _l=(ic == n_ic - 1), _fd=fdim,
                                 _s=src: e.matmul(
                                     _ps[0:_fd, _nh * 512:(_nh + 1) * 512],
                                     _bb[:, _ic * _fd:(_ic + 1) * _fd],
                                     _s[:, _ic * 1024 + _nh * 512:
                                        _ic * 1024 + _nh * 512 + 512],
                                     start=_f, stop=_l),
                                waits=wv,
                                inc=(PE if (ic == n_ic - 1 and nh == 1) else None))
                            wv = ()
                    prev_pe[sl] = PE.n
                    out_cb(mt, ps, PE.n, bank)
                    if after_mt is not None:
                        after_mt(mti, mt)

            if upto == 90:
                fused_pass(x0, yout, None, do_ln=False)
                return
            fused_pass(x0, None, None, do_ln=True, skip_store=True)
            barrier()
            if upto == 91:
                fused_pass(x0, yout, None, do_ln=False)
                return

            def qc_out(mt, ps, pev, bank):
                p.op("act", lambda e, _mt=mt, _ps=ps: e.activation(
                    qc_sb[:, _mt * 1024:(_mt + 1) * 1024], _ps[:], AF.Identity,
                    bias=bcol(BC_QC + _mt), scale=1.0), waits=[(PE, pev)], inc=ACT)
                bank.rd(ACT)
            stream_mm_fm(wqcB, 12, 32, qc_out)
            barrier()

            def kvn_out(mt, ps, pev, bank):
                buf = qkv_tmp[mt % 2]
                p.op("act", lambda e, _ps=ps, _b=buf, _mt=mt: e.activation(
                    _b[:], _ps[:], AF.Identity, bias=bcol(BC_KVN + _mt), scale=1.0),
                    waits=[(PE, pev), (Lcc, Lcc.n)], inc=ACT)
                bank.rd(ACT)
                an = ACT.n
                p.op("sp", lambda e, _mt=mt, _b=buf: e.dma_start(
                    out=cc_in_a.ap()[_mt * 128:(_mt + 1) * 128, :], in_=_b[:]),
                    waits=[(ACT, an)], inc=Lcc, amt=16)
            stream_mm_fm(wkvcnB, 4, 32, kvn_out)
            barrier()

            def kvr_out(mt, ps, pev, bank):
                buf = qkv_tmp[0]
                p.op("act", lambda e, _ps=ps, _b=buf: e.activation(
                    _b[0:64, :], _ps[0:64, :], AF.Identity,
                    bias=bcol(BC_KVR, 64), scale=1.0),
                    waits=[(PE, pev), (Lcc, Lcc.n)], inc=ACT)
                bank.rd(ACT)
                an = ACT.n
                p.op("sp", lambda e, _b=buf: e.dma_start(
                    out=cc_in_a.ap()[512:576, :], in_=_b[0:64, :]),
                    waits=[(ACT, an)], inc=Lcc, amt=16)
            stream_mm_fm(wkvcrB, 1, 32, kvr_out, fdim=64)
            barrier()

            p.wait("pool", [(Lcc, Lcc.n)])
            p.op("pool", lambda e: e.collective_compute(
                "AllGather", ALU.bypass,
                replica_groups=[[0, 1], [2, 3], [4, 5], [6, 7]],
                ins=[cc_in_a.ap().opt()], outs=[cc_out_a.ap().opt()]), inc=CCa)
            if upto == 92:
                p.wait("sp", [(CCa, 1)])
                barrier()
                fused_pass(x0, yout, None, do_ln=False)
                return

            def qup_out(mt, ps, pev, bank):
                buf = qkv_tmp[mt % 2]
                p.op("act", lambda e, _ps=ps, _b=buf, _mt=mt: e.activation(
                    _b[:], _ps[:], AF.Identity,
                    bias=bias_scl[:, BS_QUP + _mt:BS_QUP + _mt + 1],
                    scale=1.0 / math.sqrt(Q_HEAD)),
                    waits=[(PE, pev), (Lq, Lq.n), (DVE, DVE.n)], inc=ACT)
                bank.rd(ACT)
                an = ACT.n
                p.op("sp", lambda e, _mt=mt, _b=buf: e.dma_start(
                    out=q_dram[_mt * 128:(_mt + 1) * 128, :], in_=_b[:]),
                    waits=[(ACT, an)], inc=Lq, amt=16)
            stream_mm_fm(wqupB, 48, 12, qup_out, rhs=qc_sb)

            barrier()
            if upto == 93:
                fused_pass(x0, yout, None, do_ln=False)
                return

            cga = cc_out_a.ap()
            p.wait("sp", [(CCa, 1)])
            for c in range(4):
                for r in range(2):
                    p.op("sp", lambda e, _c=c, _r=r: e.dma_start(
                        out=kvnope[:, _c * 2048 + _r * 1024:
                                   _c * 2048 + _r * 1024 + 1024],
                        in_=cga[_r * 576 + _c * 128:_r * 576 + (_c + 1) * 128, :]),
                        inc=Lonce, amt=16)
            for r in range(2):
                p.op("sp", lambda e, _r=r: e.dma_start(
                    out=kvrope[:, _r * 1024:(_r + 1) * 1024],
                    in_=cga[_r * 576 + 512:_r * 576 + 576, :]), inc=Lonce, amt=16)
            p.op("sp", lambda e: e.dma_start(out=A_st[:, 0:4096], in_=woB),
                 inc=Lonce, amt=16)
            kv_loaded = Lonce.n
            _l = Lonce.n
            p.op("pool", lambda e: e.tensor_copy(woT_sb, A_st[:, 0:4096]),
                 waits=[(Lonce, _l)], inc=POOL)
            wo_pool = POOL.n
            p.op("pool", lambda e: e.tensor_tensor(
                sqbuf[0:64, :], kvrope[:], kvrope[:], ALU.mult),
                waits=[(Lonce, kv_loaded)], inc=POOL)
            krn = POOL.n
            for nt in range(4):
                hf = nt % 2
                bank = bkDh[hf]
                wv = bank.wr() + [(POOL, krn)]
                p.op("pe", lambda e, _hf=hf, _nt=nt: e.matmul(
                    psD[0:1, _hf * 512:_hf * 512 + 512], ones[0:64, 0:1],
                    sqbuf[0:64, _nt * 512:(_nt + 1) * 512], start=True, stop=True),
                    waits=wv, inc=PE)
                pev = PE.n
                p.op("act", lambda e, _hf=hf, _nt=nt: e.copy(
                    krope_nrm[0:1, _nt * 512:(_nt + 1) * 512],
                    psD[0:1, _hf * 512:_hf * 512 + 512]),
                    waits=[(PE, pev)], inc=ACT)
                bank.rd(ACT)

            bstate = {"sq_pe": PE.n, "nr_dve": 0, "bf_act": 0}

            def bounds_rows(qn, qr, kbuf, with_rope, ldq,
                            psq=None, bkq=None, psk=None, bkk=None):
                psq = psq if psq is not None else psD
                psk = psk if psk is not None else psD
                bkqs = [bkq, bkq] if bkq is not None else bkDh
                bkks = [bkk, bkk] if bkk is not None else bkDh
                p.op("pool", lambda e, _qn=qn: e.tensor_tensor(
                    sqbuf[:, 0:1024], _qn[:], _qn[:], ALU.mult),
                    waits=[ldq, (PE, bstate["sq_pe"]), (ACT, bstate["bf_act"])],
                    inc=POOL)
                if qr is not None:
                    p.op("pool", lambda e, _qr=qr: e.tensor_tensor(
                        sqbuf[0:64, 1024:2048], _qr[:], _qr[:], ALU.mult), inc=POOL)
                qsq = POOL.n
                for nhf in range(2):
                    bank = bkqs[nhf]
                    wv = bank.wr() + [(POOL, qsq)]
                    p.op("pe", lambda e, _nh=nhf, _ps=psq: e.matmul(
                        _ps[0:1, _nh * 512:_nh * 512 + 512], ones[:, 0:1],
                        sqbuf[:, _nh * 512:(_nh + 1) * 512],
                        start=True, stop=(qr is None)), waits=wv,
                        inc=(None if qr is not None else PE))
                    if qr is not None:
                        p.op("pe", lambda e, _nh=nhf, _ps=psq: e.matmul(
                            _ps[0:1, _nh * 512:_nh * 512 + 512], ones[0:64, 0:1],
                            sqbuf[0:64, 1024 + _nh * 512:1024 + (_nh + 1) * 512],
                            start=False, stop=True), inc=PE)
                    pev = PE.n
                    p.op("dve", lambda e, _nh=nhf, _ps=psq: e.tensor_copy(
                        normrow[0:1, _nh * 512:(_nh + 1) * 512],
                        _ps[0:1, _nh * 512:_nh * 512 + 512]),
                        waits=[(PE, pev), (DVE, bstate["nr_dve"])], inc=DVE)
                    bank.rd(DVE)
                p.op("pool", lambda e, _k=kbuf: e.tensor_tensor(
                    sqbuf[:, 0:2048], _k[:], _k[:], ALU.mult),
                    waits=[(ACT, ACT.n), (DVE, DVE.n)], inc=POOL)
                ksq = POOL.n
                for nt in range(4):
                    hf = nt % 2
                    bank = bkks[hf]
                    wv = bank.wr() + [(POOL, ksq)]
                    p.op("pe", lambda e, _hf=hf, _nt=nt, _ps=psk: e.matmul(
                        _ps[0:1, _hf * 512:_hf * 512 + 512], ones[:, 0:1],
                        sqbuf[:, _nt * 512:(_nt + 1) * 512],
                        start=True, stop=True), waits=wv, inc=PE)
                    pev = PE.n
                    if with_rope:
                        p.op("dve", lambda e, _hf=hf, _nt=nt, _ps=psk:
                             e.tensor_tensor(
                                 normrow[0:1, 1024 + _nt * 512:
                                         1024 + (_nt + 1) * 512],
                                 krope_nrm[0:1, _nt * 512:(_nt + 1) * 512],
                                 _ps[0:1, _hf * 512:_hf * 512 + 512], ALU.add),
                             waits=[(PE, pev)], inc=DVE)
                        bank.rd(DVE)
                    else:
                        p.op("dve", lambda e, _hf=hf, _nt=nt, _ps=psk:
                             e.tensor_copy(
                                 normrow[0:1, 1024 + _nt * 512:
                                         1024 + (_nt + 1) * 512],
                                 _ps[0:1, _hf * 512:_hf * 512 + 512]),
                             waits=[(PE, pev)], inc=DVE)
                        bank.rd(DVE)
                kn = (DVE.n, ACT.n)
                p.op("dve", lambda e: e.reduce_max(
                    stats[0:1, 7:8], normrow[0:1, 1024:3072], axis=AX.X),
                    waits=[(ACT, kn[1]), (DVE, kn[0])], inc=DVE)
                p.op("dve", lambda e: e.tensor_scalar(
                    Bf[0:1, :], normrow[0:1, 0:1024], stats[0:1, 7:8], None,
                    ALU.mult), inc=DVE)
                bstate["nr_dve"] = DVE.n
                bn = DVE.n
                p.op("act", lambda e: e.activation(
                    Bf[0:1, :], Bf[0:1, :], AF.Ln, bias=epsc[0:1, 1:2], scale=1.0),
                    waits=[(DVE, bn)], inc=ACT)
                p.op("act", lambda e: e.activation(
                    Bf[0:1, :], Bf[0:1, :], AF.Exp, bias=0.0, scale=0.5), inc=ACT)
                bstate["sq_pe"] = PE.n
                bstate["bf_act"] = ACT.n
                return ACT.n

            def attn_heads(va, mask_ld=0):
                st_ = {"qh_pe": [0, 0], "aux_pe": [0, 0], "aux_act": [0, 0],
                       "bgr": [0, 0], "ebuf_pe": [0, 0, 0, 0], "kvs_pe": 0,
                       "rowbuf_pe": 0, "bcast_dve": 0, "sums_pe": 0,
                       "recip_dve": 0, "ctx_pe": 0}
                wkv_v2 = wkvupB.rearrange("(s p) c -> s p c", p=128)
                cgb2 = cc_out_b.ap()
                if va == "a":
                    p.op("dve", lambda e: e.memset(ctxacc, 0.0), inc=DVE)
                acc0_dve = DVE.n

                def phase1(h):
                    pr = h % 2
                    qn = qh_n[pr]
                    khb = kh_a[pr] if va == "a" else kh_b[pr]
                    vhb = vh_a[pr] if va == "a" else vh_b[pr]
                    qsrc = q_dram if va == "a" else qb_dram
                    qwv = [(PE, st_["qh_pe"][pr])]
                    if va == "b":
                        qwv.append((Lq, Lq.n))
                    p.op("sp", lambda e, _h=h, _qn=qn, _qs=qsrc: e.dma_start(
                        out=_qn[:], in_=_qs[_h * 128:(_h + 1) * 128, :]),
                        waits=qwv, inc=Lh[pr], amt=16)
                    if va == "a":
                        qr = qh_r[pr]
                        p.op("sp", lambda e, _h=h, _qr=qr: e.dma_start(
                            out=_qr[:],
                            in_=q_dram[4096 + _h * 64:4096 + (_h + 1) * 64, :]),
                            inc=Lh[pr], amt=16)
                        kb, vb = kvs_ld
                        kbb, vbb = kvs_bf
                        p.op("sp", lambda e, _h=h, _kb=kb: e.dma_start(
                            out=_kb, in_=wkv_v2[2 * _h]),
                            waits=[(POOL, POOL.n)], inc=Lh[pr], amt=16)
                        p.op("sp", lambda e, _h=h, _vb=vb: e.dma_start(
                            out=_vb, in_=wkv_v2[2 * _h + 1]), inc=Lh[pr], amt=16)
                        ldall = Lh[pr].n
                        ldq = (Lh[pr], ldall)
                        p.op("pool", lambda e: e.tensor_copy(kbb, kb),
                             waits=[(Lh[pr], ldall), (PE, st_["kvs_pe"])],
                             inc=POOL)
                        p.op("pool", lambda e: e.tensor_copy(vbb, vb), inc=POOL)
                        cn = POOL.n
                        p.op("dve", lambda e, _h=h: e.tensor_copy(
                            colbuf[:], bcol(BC_KVUP + 2 * _h + 1)),
                            waits=[(PE, st_["rowbuf_pe"])], inc=DVE)
                        cbn = DVE.n
                        wv = bkA.wr() + [(DVE, cbn)]
                        p.op("pe", lambda e: e.matmul(
                            psA[0:1, 0:128], colbuf[:], ident[:],
                            start=True, stop=True), waits=wv, inc=PE)
                        rb_pe = PE.n
                        p.op("dve", lambda e: e.tensor_copy(
                            rowbuf[:], psA[0:1, 0:128]),
                            waits=[(PE, rb_pe)], inc=DVE)
                        bkA.rd(DVE)
                        rb_act = DVE.n
                        for half in range(2):
                            ps, bank = (psA, bkA) if half == 0 else (psB, bkB)
                            wv = bank.wr() + [(POOL, cn), (Lonce, kv_loaded)]
                            for nhf in range(2):
                                nt = half * 2 + nhf
                                for ic in range(4):
                                    p.op("pe", lambda e, _ps=ps, _nh=nhf, _nt=nt,
                                         _ic=ic: e.matmul(
                                             _ps[:, _nh * 512:(_nh + 1) * 512],
                                             kvs_bf[0][:, _ic * 128:
                                                       (_ic + 1) * 128],
                                             kvnope[:, _ic * 2048 + _nt * 512:
                                                    _ic * 2048 + _nt * 512 + 512],
                                             start=(_ic == 0), stop=(_ic == 3)),
                                        waits=wv,
                                        inc=(PE if (ic == 3 and nhf == 1)
                                             else None))
                                    wv = ()
                            pev = PE.n
                            p.op("dve", lambda e, _ps=ps, _half=half, _k=khb,
                                 _h=h: e.tensor_scalar(
                                     _k[:, _half * 1024:(_half + 1) * 1024],
                                     _ps[:], bcol(BC_KVUP + 2 * _h), None,
                                     ALU.add),
                                 waits=[(PE, pev)], inc=DVE)
                            bank.rd(DVE)
                        for g in range(4):
                            ps, bank = (psA, bkA) if g % 2 == 0 else (psB, bkB)
                            wv = bank.wr() + [(DVE, rb_act)]
                            for j in range(4):
                                kt = g * 4 + j
                                p.op("pe", lambda e, _ps=ps, _j=j: e.matmul(
                                    _ps[:, _j * 128:(_j + 1) * 128],
                                    ones[0:1, 0:128], rowbuf[0:1, :],
                                    start=True, stop=False), waits=wv)
                                wv = ()
                                for ic in range(4):
                                    p.op("pe", lambda e, _ps=ps, _j=j, _kt=kt,
                                         _ic=ic: e.matmul(
                                             _ps[:, _j * 128:(_j + 1) * 128],
                                             kvnope[:, _ic * 2048 + _kt * 128:
                                                    _ic * 2048 + (_kt + 1) * 128],
                                             kvs_bf[1][:, _ic * 128:
                                                       (_ic + 1) * 128],
                                             start=False, stop=(_ic == 3)),
                                        inc=(PE if (ic == 3 and j == 3)
                                             else None))
                            pev = PE.n
                            p.op("dve", lambda e, _ps=ps, _g=g, _v=vhb:
                                 e.tensor_copy(
                                     _v[:, _g * 512:(_g + 1) * 512],
                                     _ps[:, 0:512]),
                                 waits=[(PE, pev)], inc=DVE)
                            bank.rd(DVE)
                        st_["kvs_pe"] = PE.n
                        st_["rowbuf_pe"] = PE.n
                        ban = bounds_rows(qn, qr, khb, True, ldq,
                                          psq=psA, bkq=bkA, psk=psB, bkk=bkB)
                        p.op("dve", lambda e, _pr=pr: e.tensor_scalar(
                            Bg[0:1, _pr * 1024:(_pr + 1) * 1024], Bf[0:1, :],
                            1.0, None, ALU.mult),
                            waits=[(ACT, ban), (PE, PE.n)], inc=DVE)
                        st_["bgr"][pr] = DVE.n
                    else:
                        vfb = vhf_b[pr]
                        for r in range(2):
                            p.op("sp", lambda e, _h=h, _r=r, _k=khb: e.dma_start(
                                out=_k[:, _r * 1024:(_r + 1) * 1024],
                                in_=cgb2[_r * 8192 + _h * 128:
                                         _r * 8192 + (_h + 1) * 128, :]),
                                waits=[(PE, st_["aux_pe"][pr]), (POOL, POOL.n)],
                                inc=Lh[pr], amt=16)
                            p.op("sp", lambda e, _h=h, _r=r, _v=vfb: e.dma_start(
                                out=_v[:, _r * 1024:(_r + 1) * 1024],
                                in_=cgb2[_r * 8192 + 4096 + _h * 128:
                                         _r * 8192 + 4096 + (_h + 1) * 128, :]),
                                inc=Lh[pr], amt=16)
                        ldall = Lh[pr].n
                        ldq = (Lh[pr], ldall)
                        for g in range(4):
                            ps, bank = (psA, bkA) if g % 2 == 0 else (psB, bkB)
                            wv = bank.wr() + [(Lh[pr], ldall),
                                              (PE, st_["aux_pe"][pr])]
                            for j in range(4):
                                kt = g * 4 + j
                                p.op("pe", lambda e, _ps=ps, _j=j, _kt=kt,
                                     _v=vfb: e.matmul(
                                         _ps[:, _j * 128:(_j + 1) * 128],
                                         _v[:, _kt * 128:(_kt + 1) * 128],
                                         ident[:], start=True, stop=True),
                                    waits=wv, inc=(PE if j == 3 else None))
                                wv = ()
                            pev = PE.n
                            p.op("dve", lambda e, _ps=ps, _g=g, _v=vhb:
                                 e.tensor_copy(
                                     _v[:, _g * 512:(_g + 1) * 512],
                                     _ps[:, 0:512]),
                                 waits=[(PE, pev)], inc=DVE)
                            bank.rd(DVE)
                        ban = bounds_rows(qn, None, khb, False, ldq,
                                          psq=psA, bkq=bkA, psk=psB, bkk=bkB)
                        sl_h = 2.0 ** (-0.25 * (h + 1))
                        p.op("dve", lambda e, _s=sl_h: e.scalar_tensor_tensor(
                            Bf[0:1, :], qglob[0:1, :], _s, Bf[0:1, :],
                            ALU.mult, ALU.add),
                            waits=[(ACT, ban), (PE, PE.n)], inc=DVE)
                        p.op("dve", lambda e, _pr=pr: e.tensor_scalar(
                            Bg[0:1, _pr * 1024:(_pr + 1) * 1024], Bf[0:1, :],
                            1.0, None, ALU.mult), inc=DVE)
                        p.op("dve", lambda e, _s=sl_h, _pr=pr: e.tensor_scalar(
                            acol[:, _pr * 16:(_pr + 1) * 16], kidx[:],
                            _s, None, ALU.mult),
                            waits=[(ACT, st_["aux_act"][pr])], inc=DVE)
                        st_["bgr"][pr] = DVE.n
                    st_["aux_act"][pr] = ACT.n

                def phase2(h, inject=None):
                    pr = h % 2
                    qn = qh_n[pr]
                    qr = qh_r[pr] if va == "a" else None
                    khb = kh_a[pr] if va == "a" else kh_b[pr]
                    vhb = vh_a[pr] if va == "a" else vh_b[pr]
                    aux = st_["aux_act"][pr]
                    expk_of = {}

                    def consume(kt):
                        slot = kt % 4
                        expk = expk_of.pop(kt)
                        wv1 = (bkDh[0].wr() + bkDh[1].wr()) if kt == 0 else []
                        wv2 = bkC.wr() if kt == 0 else []
                        for q5 in range(2):
                            p.op("pe", lambda e, _sl=slot, _q5=q5,
                                 _kt=kt: e.matmul(
                                     psD[0:1, _q5 * 512:_q5 * 512 + 512],
                                     ones[:, 0:1],
                                     Ebuf[:, _sl * 1024 + _q5 * 512:
                                          _sl * 1024 + _q5 * 512 + 512],
                                     start=(_kt == 0), stop=(_kt == 15)),
                                waits=(wv1 + [(ACT, expk)]),
                                inc=(PE if (kt == 15 and q5 == 1) else None))
                            wv1 = []
                        for q5 in range(2):
                            p.op("pe", lambda e, _sl=slot, _q5=q5, _kt=kt,
                                 _v=vhb: e.matmul(
                                     psC[:, _q5 * 512:(_q5 + 1) * 512],
                                     _v[:, _kt * 128:(_kt + 1) * 128],
                                     Ebuf[:, _sl * 1024 + _q5 * 512:
                                          _sl * 1024 + _q5 * 512 + 512],
                                     start=(_kt == 0), stop=(_kt == 15)),
                                waits=wv2, inc=(PE if q5 == 1 else None))
                            wv2 = []
                        st_["ebuf_pe"][slot] = PE.n

                    for kt in range(16):
                        if kt == 4 and inject is not None:
                            inject()
                        slot = kt % 4
                        ps, bank = (psA, bkA) if kt % 2 == 0 else (psB, bkB)
                        wv = bank.wr() + [(DVE, st_["bgr"][pr]), (ACT, aux)]
                        if va == "b":
                            wv = wv + [(Lonce, mask_ld)]
                        for q5 in range(2):
                            p.op("pe", lambda e, _ps=ps, _kt=kt, _q5=q5, _qn=qn,
                                 _k=khb: e.matmul(
                                     _ps[:, _q5 * 512:(_q5 + 1) * 512],
                                     _k[:, _kt * 128:(_kt + 1) * 128],
                                     _qn[:, _q5 * 512:(_q5 + 1) * 512],
                                     start=True, stop=False), waits=wv)
                            wv = ()
                        if va == "a":
                            for q5 in range(2):
                                p.op("pe", lambda e, _ps=ps, _kt=kt, _q5=q5,
                                     _qr=qr: e.matmul(
                                         _ps[:, _q5 * 512:(_q5 + 1) * 512],
                                         kvrope[:, _kt * 128:(_kt + 1) * 128],
                                         _qr[:, _q5 * 512:(_q5 + 1) * 512],
                                         start=False, stop=False))
                        rank_stop = (va == "a")
                        for q5 in range(2):
                            p.op("pe", lambda e, _ps=ps, _q5=q5, _pr=pr,
                                 _st=rank_stop: e.matmul(
                                     _ps[:, _q5 * 512:(_q5 + 1) * 512],
                                     negones[0:1, 0:128],
                                     Bg[0:1, _pr * 1024 + _q5 * 512:
                                        _pr * 1024 + _q5 * 512 + 512],
                                     start=False, stop=_st),
                                inc=(PE if (rank_stop and q5 == 1) else None))
                        if va == "b":
                            for q5 in range(2):
                                p.op("pe", lambda e, _ps=ps, _kt=kt,
                                     _q5=q5: e.matmul(
                                         _ps[:, _q5 * 512:(_q5 + 1) * 512],
                                         ident[:],
                                         maskv[:, _q5 * 8192 + _kt * 512:
                                               _q5 * 8192 + (_kt + 1) * 512],
                                         start=False, stop=True),
                                    inc=(PE if q5 == 1 else None))
                        pev = PE.n
                        bias = 0.0 if va == "a" else acol[:, pr * 16 + kt:
                                                          pr * 16 + kt + 1]
                        p.op("act", lambda e, _ps=ps, _sl=slot, _bi=bias:
                             e.activation(
                                 Ebuf[:, _sl * 1024:(_sl + 1) * 1024], _ps[:],
                                 AF.Exp, bias=_bi, scale=1.0),
                             waits=[(PE, pev), (PE, st_["ebuf_pe"][slot])],
                             inc=ACT)
                        bank.rd(ACT)
                        expk_of[kt] = ACT.n
                        if kt > 0:
                            consume(kt - 1)
                    consume(15)
                    st_["sums_pe"] = PE.n
                    st_["ctx_pe"] = PE.n

                    def _recipu(e):
                        with nc.allow_low_precision(
                                reason="bf16 softmax normalizer"):
                            return e.reciprocal(rsum[0:1, 0:1024],
                                                psD[0:1, 0:1024])
                    p.op("dve", _recipu, waits=[(PE, st_["sums_pe"])], inc=DVE)
                    bkDh[0].rd(DVE)
                    bkDh[1].rd(DVE)
                    st_["recip_dve"] = DVE.n
                    st_["qh_pe"][pr] = PE.n
                    st_["aux_pe"][pr] = PE.n

                def phase3(h):
                    wv = bkDh[0].wr() + bkDh[1].wr() + [(DVE, st_["recip_dve"])]
                    for q5 in range(2):
                        p.op("pe", lambda e, _q5=q5: e.matmul(
                            psD[:, _q5 * 512:(_q5 + 1) * 512], ones[0:1, 0:128],
                            rsum[0:1, _q5 * 512:(_q5 + 1) * 512],
                            start=True, stop=True), waits=wv,
                            inc=(PE if q5 == 1 else None))
                        wv = []
                    bb_pe = PE.n
                    p.op("act", lambda e: e.copy(bcast[:], psD[:]),
                         waits=[(PE, bb_pe), (DVE, st_["bcast_dve"])], inc=ACT)
                    bkDh[0].rd(ACT)
                    bkDh[1].rd(ACT)
                    bc_act = ACT.n
                    if va == "b":
                        p.op("dve", lambda e, _h=h: e.tensor_tensor(
                            A_xhat[:, _h * 1024:(_h + 1) * 1024], psC[:],
                            bcast[:], ALU.mult),
                            waits=[(ACT, bc_act), (PE, st_["ctx_pe"])], inc=DVE)
                        bkC.rd(DVE)
                    else:
                        p.op("dve", lambda e: e.tensor_tensor(
                            bcast[:], psC[:], bcast[:], ALU.mult),
                            waits=[(ACT, bc_act), (PE, st_["ctx_pe"]),
                                   (DVE, acc0_dve)], inc=DVE)
                        bkC.rd(DVE)
                        p.op("dve", lambda e: e.tensor_tensor(
                            ctxacc, ctxacc, bcast[:], ALU.add), inc=DVE)
                    st_["bcast_dve"] = DVE.n

                phase1(0)
                for h in range(NH):
                    phase2(h)
                    if h + 1 < NH:
                        phase1(h + 1)
                    phase3(h)
                return DVE.n

            acc_dve = attn_heads("a")
            p.op("act", lambda e: e.copy(ctxa[:], ctxacc),
                 waits=[(DVE, acc_dve)], inc=ACT)
            ctx_act = ACT.n
            if upto == 94:
                barrier()
                fused_pass(x0, yout, None, do_ln=False)
                return

            prev_act_oa = [0, 0]
            for mt in range(32):
                ps, bank = (psA, bkA) if mt % 2 == 0 else (psB, bkB)
                wv = bank.wr() + [(ACT, ctx_act)]
                for nh in range(2):
                    p.op("pe", lambda e, _ps=ps, _nh=nh, _mt=mt: e.matmul(
                        _ps[:, _nh * 512:(_nh + 1) * 512],
                        woT_sb[:, _mt * 128:(_mt + 1) * 128],
                        ctxa[:, _nh * 512:(_nh + 1) * 512],
                        start=True, stop=True),
                        waits=wv, inc=(PE if nh == 1 else None))
                    wv = ()
                pev = PE.n
                p.op("act", lambda e, _ps=ps, _mt=mt: e.activation(
                    A_y[:, _mt * 1024:(_mt + 1) * 1024], _ps[:], AF.Identity,
                    bias=bcol(BC_BO + _mt), scale=1.0), waits=[(PE, pev)], inc=ACT)
                bank.rd(ACT)
            yd_a = ACT.n
            if upto == 95:
                barrier()
                fused_pass(x0, yout, None, do_ln=False)
                return

            def mk_psrc(yd):
                def psrc(tt, ft):
                    hf = ft % 2
                    bank = bkDh[hf]
                    wv = bank.wr() + [(ACT, yd)]
                    for j in range(4):
                        oc = ft * 4 + j
                        p.op("pe", lambda e, _hf=hf, _j=j, _oc=oc,
                             _tt=tt: e.matmul(
                                 psD[:, _hf * 512 + _j * 128:
                                     _hf * 512 + (_j + 1) * 128],
                                 A_y[:, _oc * 1024 + _tt * 128:
                                     _oc * 1024 + (_tt + 1) * 128],
                                 identh[:], start=True, stop=True),
                            waits=wv, inc=(PE if j == 3 else None))
                        wv = ()
                    return (psD[:, hf * 512:(hf + 1) * 512], PE.n, bank)
                return psrc

            barrier()
            if upto == 1:
                fused_pass(x0, yout, mk_psrc(yd_a), do_ln=False)
            else:
                fused_pass(x0, xA, mk_psrc(yd_a), do_ln=True)

            def ffn_phase(src_x, dst_x, final):
                barrier()
                wff = wffnB.rearrange("(m p) c -> m p c", p=128)
                srp = srepB.rearrange("(m p) c -> m p c", p=128)
                prev_pool = [0, 0]
                prev_pe = [0, 0]
                for ot in range(32):
                    sl = ot % 2
                    lw = A_stH.ap()[:, sl * 8192:sl * 8192 + 4096]
                    ls = A_stH.ap()[:, sl * 8192 + 4096:sl * 8192 + 8192]
                    bb = A_sb.ap()[:, sl * 4096:(sl + 1) * 4096]
                    p.op("sp", lambda e, _ot=ot, _lw=lw: e.dma_start(
                        out=_lw, in_=wff[_ot]), waits=[(POOL, prev_pool[sl])],
                        inc=Lw[sl], amt=16)
                    p.op("sp", lambda e, _ot=ot, _ls=ls: e.dma_start(
                        out=_ls, in_=srp[_ot]), inc=Lw[sl], amt=16)
                    ldn = Lw[sl].n
                    p.op("pool", lambda e, _lw=lw, _ls=ls, _bb=bb: e.tensor_tensor(
                        _bb, _lw, _ls, ALU.mult),
                        waits=[(Lw[sl], ldn), (PE, prev_pe[sl])], inc=POOL)
                    prev_pool[sl] = POOL.n
                    cn = POOL.n
                    ps, bank = (psA, bkA) if sl == 0 else (psB, bkB)
                    wv = bank.wr() + [(POOL, cn)]
                    for nh in range(2):
                        for ic in range(32):
                            p.op("pe", lambda e, _ps=ps, _nh=nh, _bb=bb,
                                 _ic=ic: e.matmul(
                                     _ps[:, _nh * 512:(_nh + 1) * 512],
                                     _bb[:, _ic * 128:(_ic + 1) * 128],
                                     A_xhat[:, _ic * 1024 + _nh * 512:
                                            _ic * 1024 + _nh * 512 + 512],
                                     start=(_ic == 0), stop=(_ic == 31)),
                                waits=wv,
                                inc=(PE if (ic == 31 and nh == 1) else None))
                            wv = ()
                    prev_pe[sl] = PE.n
                    pev = PE.n
                    p.op("act", lambda e, _ps=ps, _ot=ot: e.copy(
                        A_y[:, _ot * 1024:(_ot + 1) * 1024], _ps[:]),
                        waits=[(PE, pev)], inc=ACT)
                    bank.rd(ACT)
                yd = ACT.n
                barrier()
                fused_pass(src_x, dst_x, mk_psrc(yd), do_ln=not final)

            if upto >= 2:
                ffn_phase(xA, yout if upto == 2 else xB, final=(upto == 2))

            def attn_b_phase(src_x, dst_x, final):
                barrier()

                def qkv_out(mt, ps, pev, bank):
                    buf = qkv_tmp[mt % 2]
                    if mt < 32:
                        bias = bias_scl[:, BS_QKVQ + mt:BS_QKVQ + mt + 1]
                        scale = HD ** -0.5
                        dst = qb_dram[mt * 128:(mt + 1) * 128, :]
                        sem = Lq
                    else:
                        bias = bcol(BC_QKV + mt)
                        scale = 1.0
                        sem = Lqb
                        if mt < 64:
                            dst = cc_in_b.ap()[(mt - 32) * 128:(mt - 31) * 128, :]
                        else:
                            dst = cc_in_b.ap()[4096 + (mt - 64) * 128:
                                               4096 + (mt - 63) * 128, :]
                    p.op("act", lambda e, _ps=ps, _b=buf, _bi=bias, _s=scale:
                         e.activation(_b[:], _ps[:], AF.Identity, bias=_bi, scale=_s),
                         waits=[(PE, pev), (Lqb, Lqb.n), (Lq, Lq.n),
                                (DVE, DVE.n)], inc=ACT)
                    bank.rd(ACT)
                    an = ACT.n
                    p.op("sp", lambda e, _b=buf, _d=dst: e.dma_start(
                        out=_d, in_=_b[:]), waits=[(ACT, an)], inc=sem, amt=16)
                def qkv_after(mti, mt):
                    if mti == 63:
                        p.wait("pool", [(Lqb, Lqb.n)])
                        p.op("pool", lambda e: e.collective_compute(
                            "AllGather", ALU.bypass,
                            replica_groups=[[0, 1], [2, 3], [4, 5], [6, 7]],
                            ins=[cc_in_b.ap().opt()],
                            outs=[cc_out_b.ap().opt()]), inc=CCb)
                stream_mm_fm(wqkvB, 96, 32, qkv_out,
                             mt_order=list(range(32, 96)) + list(range(32)),
                             after_mt=qkv_after)
                p.wait("sp", [(ACT, ACT.n), (DVE, DVE.n), (PE, PE.n)])
                p.op("sp", lambda e: e.dma_start(out=maskv, in_=maskD),
                     inc=Lonce, amt=16)
                mask_ld = Lonce.n
                p.wait("sp", [(CCb, 1)])

                attn_heads("b", mask_ld=mask_ld)
                barrier()

                wvo = woutB.rearrange("(m p) c -> m p c", p=128)
                prev_pool = [0, 0]
                prev_pe = [0, 0]
                for mt in range(32):
                    sl = mt % 2
                    lb = A_st.ap()[:, sl * 4096:(sl + 1) * 4096]
                    bb = A_sb.ap()[:, sl * 4096:(sl + 1) * 4096]
                    p.op("sp", lambda e, _mt=mt, _lb=lb: e.dma_start(
                        out=_lb, in_=wvo[_mt]), waits=[(POOL, prev_pool[sl])],
                        inc=Lw[sl], amt=16)
                    ldn = Lw[sl].n
                    p.op("pool", lambda e, _lb=lb, _bb=bb: e.tensor_copy(_bb, _lb),
                         waits=[(Lw[sl], ldn), (PE, prev_pe[sl])], inc=POOL)
                    prev_pool[sl] = POOL.n
                    cn = POOL.n
                    ps, bank = (psA, bkA) if sl == 0 else (psB, bkB)
                    wv = bank.wr() + [(POOL, cn)]
                    for nh in range(2):
                        for ic in range(32):
                            p.op("pe", lambda e, _ps=ps, _nh=nh, _bb=bb,
                                 _ic=ic: e.matmul(
                                     _ps[:, _nh * 512:(_nh + 1) * 512],
                                     _bb[:, _ic * 128:(_ic + 1) * 128],
                                     A_xhat[:, _ic * 1024 + _nh * 512:
                                            _ic * 1024 + _nh * 512 + 512],
                                     start=(_ic == 0), stop=(_ic == 31)),
                                waits=wv,
                                inc=(PE if (ic == 31 and nh == 1) else None))
                            wv = ()
                    prev_pe[sl] = PE.n
                    pev = PE.n
                    p.op("act", lambda e, _ps=ps, _mt=mt: e.activation(
                        A_y[:, _mt * 1024:(_mt + 1) * 1024], _ps[:], AF.Identity,
                        bias=bcol(BC_OUT + _mt), scale=1.0),
                        waits=[(PE, pev)], inc=ACT)
                    bank.rd(ACT)
                yd = ACT.n
                barrier()
                fused_pass(src_x, dst_x, mk_psrc(yd), do_ln=not final)

            if upto >= 3:
                attn_b_phase(xB, yout if upto == 3 else xA, final=(upto == 3))
            if upto >= 4:
                ffn_phase(xA, yout, final=True)

            barrier()
            p.wait("sp", [(STO[0], STO[0].n), (STO[1], STO[1].n)])

        _emit()


        with nc.Block() as block:
            @block.tensor
            def _(e):
                for fn in p.ops["pe"]:
                    fn(e)

            @block.scalar
            def _(e):
                for fn in p.ops["act"]:
                    fn(e)

            @block.vector
            def _(e):
                for fn in p.ops["dve"]:
                    fn(e)

            @block.gpsimd
            def _(e):
                for fn in p.ops["pool"]:
                    fn(e)

            @block.sync
            def _(e):
                for fn in p.ops["sp"]:
                    fn(e)

    return nc



def _prep_inputs(d):
    out = {}
    out["wqcB"] = _blocked(np.ascontiguousarray(d["L0_attn_a_Wq_c"].T), 32, 12) \
        .reshape(12 * 128, 32 * 128)
    perm = np.concatenate(
        [np.arange(h * 192, h * 192 + 128) for h in range(NH)]
        + [np.arange(h * 192 + 128, h * 192 + 192) for h in range(NH)])
    out["wqupB"] = _blocked(
        np.ascontiguousarray(d["L0_attn_a_Wq_up"][perm].T), 12, 48) \
        .reshape(48 * 128, 12 * 128)
    wkvc = np.asarray(d["L0_attn_a_Wkv_c"])
    out["wkvcnB"] = _blocked(np.ascontiguousarray(wkvc[:512].T), 32, 4) \
        .reshape(4 * 128, 32 * 128)
    out["wkvcrB"] = _blocked(np.ascontiguousarray(wkvc[512:576].T), 32, 1, f=64) \
        .reshape(128, 32 * 64)
    kvupT = np.ascontiguousarray(np.asarray(d["L0_attn_a_Wkv_up"]).T)
    sl = np.zeros((64, 128, 512), dtype=np.float32)
    for h in range(NH):
        for i, (a, b) in enumerate([(h * 256, h * 256 + 128),
                                    (h * 256 + 128, h * 256 + 256)]):
            sl[2 * h + i] = kvupT[:, a:b].reshape(4, 128, 128) \
                .transpose(1, 0, 2).reshape(128, 512)
    out["wkvupB"] = np.ascontiguousarray(sl.reshape(64 * 128, 512))
    out["woB"] = np.ascontiguousarray(np.asarray(d["L0_attn_a_Wo"]).T)
    out["wqkvB"] = _blocked(
        np.ascontiguousarray(np.asarray(d["L0_attn_b_w_qkv"])), 32, 96) \
        .reshape(96 * 128, 32 * 128)
    out["woutB"] = _blocked(
        np.ascontiguousarray(np.asarray(d["L0_attn_b_w_out"])), 32, 32) \
        .reshape(32 * 128, 32 * 128)
    out["wffnB"] = _blocked(
        np.ascontiguousarray(np.asarray(d["L0_ffn_weight"]).T), 32, 32) \
        .reshape(32 * 128, 32 * 128)
    sc = np.asarray(d["L0_ffn_scales"])
    s2 = np.ascontiguousarray(sc.reshape(32, 128, 32).transpose(0, 2, 1))
    out["srepB"] = np.ascontiguousarray(
        np.broadcast_to(s2[:, None, :, :], (32, 128, 32, 128))) \
        .reshape(32 * 128, 32 * 128)

    bq_up_perm = np.asarray(d["L0_attn_a_bq_up"])[perm]
    bkvup_sl = np.zeros(8192, dtype=np.float32)
    for h in range(NH):
        bkvup_sl[2 * h * 128:(2 * h + 1) * 128] = \
            d["L0_attn_a_bkv_up"][h * 256:h * 256 + 128]
        bkvup_sl[(2 * h + 1) * 128:(2 * h + 2) * 128] = \
            d["L0_attn_a_bkv_up"][h * 256 + 128:h * 256 + 256]
    cols = np.zeros((128, BC_N), dtype=np.float32)

    def put(base, vec):
        v = np.asarray(vec, dtype=np.float32).reshape(-1)
        nc_ = len(v) // 128 if len(v) % 128 == 0 else None
        if nc_ is None:
            assert len(v) == 64
            cols[0:64, base] = v
        else:
            cols[:, base:base + nc_] = v.reshape(nc_, 128).T
    put(BC_QC, d["L0_attn_a_bq_c"])
    put(BC_QUP, bq_up_perm)
    put(BC_KVN, d["L0_attn_a_bkv_c"][:512])
    put(BC_KVR, d["L0_attn_a_bkv_c"][512:576])
    put(BC_KVUP, bkvup_sl)
    put(BC_BO, d["L0_attn_a_bo"])
    put(BC_QKV, d["L0_attn_b_b_qkv"])
    put(BC_OUT, d["L0_attn_b_b_out"])
    out["biasD"] = np.ascontiguousarray(cols)

    out["identB"] = _bf(np.eye(128, dtype=np.float32))
    out["identH"] = np.ascontiguousarray(np.eye(128, dtype=np.float16))
    out["kidxD"] = np.ascontiguousarray(
        (np.arange(128)[:, None] + 128.0 * np.arange(16)[None, :])
        .astype(np.float32))
    return out


def kernel(**inputs):
    d = inputs
    prep = _prep_inputs(d)
    x = np.ascontiguousarray(np.asarray(d["L0_attn_a_hidden_size"],
                                        dtype=np.float32))
    xsh = x.reshape(B_, 2, T, D)

    nc = build_nc(_UPTO)
    in_maps = []
    kkr = np.arange(128)
    qqr = np.arange(512)
    for c in range(8):
        b, half = c // 2, c % 2
        m = dict(prep)
        m["x0"] = np.ascontiguousarray(xsh[b, half])
        off = half * T
        mask = np.zeros((2, 16, 128, 512), dtype=np.float32)
        for q5 in range(2):
            for kt in range(16):
                kg = kt * 128 + kkr[:, None]
                qg = off + q5 * 512 + qqr[None, :]
                mask[q5, kt] = np.where(kg <= qg, 0.0, MASKVAL)
        m["maskD"] = _bf(np.ascontiguousarray(
            mask.transpose(2, 0, 1, 3)).reshape(128, 16384))
        m["qglobD"] = np.ascontiguousarray(
            (off + np.arange(T, dtype=np.float32))[None, :])
        in_maps.append(m)

    res = run_bass_kernel_spmd(nc, in_maps, core_ids=list(range(8)))
    outs = np.stack([np.stack([res.results[2 * b + h]["yout"]
                               for h in range(2)]) for b in range(B_)])
    return np.ascontiguousarray(outs.reshape(B_, S, D)).astype(np.float32)


# version 08
# speedup vs baseline: 1.3280x; 1.3280x over previous
import math
import numpy as np
import ml_dtypes

import concourse.bass as bass
import concourse.mybir as mybir
from concourse.bass_utils import run_bass_kernel_spmd

F32 = mybir.dt.float32
F16 = mybir.dt.float16
BF16 = mybir.dt.bfloat16
AF = mybir.ActivationFunctionType
ALU = mybir.AluOpType
AX = mybir.AxisListType

NH = 32
Q_HEAD = 192
D, S, B_, T, HD = 4096, 2048, 4, 1024, 128
MASKVAL = -1992.0

_UPTO = 4

BC_QC = 0
BC_QUP = 12
BC_KVN = 60
BC_KVR = 64
BC_KVUP = 65
BC_BO = 129
BC_QKV = 161
BC_OUT = 257
BC_N = 289
BS_QUP = 0
BS_QKVQ = 48
BS_N = 80


def _bf(x):
    return np.ascontiguousarray(np.asarray(x).astype(ml_dtypes.bfloat16))


def _blocked(wt, ic, mt, f=128):
    return np.ascontiguousarray(wt.reshape(ic, 128, mt, f).transpose(2, 1, 0, 3))


class Ctr:
    def __init__(self, stack, nc, name):
        self.sem = stack.enter_context(nc.semaphore(name))
        self.n = 0


class Bank:
    def __init__(self):
        self.marks = []

    def wr(self):
        m, self.marks = self.marks, []
        return m

    def rd(self, ctr):
        self.marks.append((ctr, ctr.n))


class Prog:

    def __init__(self):
        self.ops = {k: [] for k in ("pe", "act", "dve", "pool", "sp")}
        self.waited = {k: {} for k in self.ops}
        self.self_ctr = {}

    def op(self, eng, build, waits=(), inc=None, amt=1):
        sc = self.self_ctr.get(eng)
        if sc is not None:
            waits = list(waits) + [(sc, sc.n)]
            if inc is None:
                inc, amt = sc, 1
        if inc is False:
            inc = None
        wl = []
        for (c, v) in waits:
            if v <= 0:
                continue
            key = id(c)
            if self.waited[eng].get(key, -1) >= v:
                continue
            self.waited[eng][key] = v
            wl.append((c.sem, v))
        sem = None
        if inc is not None:
            inc.n += amt
            sem = inc.sem

        def fn(e, _wl=wl, _b=build, _sem=sem, _amt=amt):
            for (s, v) in _wl:
                e.wait_ge(s, v)
            ins = _b(e)
            if _sem is not None:
                ins.then_inc(_sem, _amt)
        self.ops[eng].append(fn)

    def wait(self, eng, waits):
        self.op(eng, lambda e: e.nop(), waits=waits, inc=False)


def build_nc(upto=None):
    if upto is None:
        upto = _UPTO
    nc = bass.Bass("TRN2", target_bir_lowering=False)
    dt = nc.dram_tensor

    x0 = dt("x0", [T, D], F32, kind="ExternalInput").ap()
    wqcB = dt("wqcB", [12 * 128, 32 * 128], F32, kind="ExternalInput").ap()
    wqupB = dt("wqupB", [48 * 128, 12 * 128], F32, kind="ExternalInput").ap()
    wkvcnB = dt("wkvcnB", [4 * 128, 32 * 128], F32, kind="ExternalInput").ap()
    wkvcrB = dt("wkvcrB", [128, 32 * 64], F32, kind="ExternalInput").ap()
    wkvupB = dt("wkvupB", [64 * 128, 512], F32, kind="ExternalInput").ap()
    woB = dt("woB", [128, D], F32, kind="ExternalInput").ap()
    wqkvB = dt("wqkvB", [96 * 128, 32 * 128], F32, kind="ExternalInput").ap()
    woutB = dt("woutB", [32 * 128, 32 * 128], F32, kind="ExternalInput").ap()
    wffnB = dt("wffnB", [32 * 128, 32 * 128], F16, kind="ExternalInput").ap()
    srepB = dt("srepB", [32 * 128, 32 * 128], F16, kind="ExternalInput").ap()
    biasD = dt("biasD", [128, BC_N], F32, kind="ExternalInput").ap()
    identB = dt("identB", [128, 128], BF16, kind="ExternalInput").ap()
    identH = dt("identH", [128, 128], F16, kind="ExternalInput").ap()
    maskD = dt("maskD", [128, 16384], BF16, kind="ExternalInput").ap()
    kidxD = dt("kidxD", [128, 16], F32, kind="ExternalInput").ap()
    qglobD = dt("qglobD", [1, T], F32, kind="ExternalInput").ap()

    yout = dt("yout", [T, D], F32, kind="ExternalOutput").ap()

    xA = dt("xA", [T, D], F32).ap()
    xB = dt("xB", [T, D], F32).ap()
    q_dram = dt("q_dram", [6144, T], BF16).ap()
    cc_in_a = dt("cc_in_a", [576, T], BF16)
    cc_out_a = dt("cc_out_a", [2 * 576, T], BF16)
    qb_dram = dt("qb_dram", [4096, T], BF16).ap()
    cc_in_b = dt("cc_in_b", [8192, T], BF16)
    cc_out_b = dt("cc_out_b", [2 * 8192, T], BF16)

    from contextlib import ExitStack
    with ExitStack() as st:
        Ec = st.enter_context

        A_xhat = Ec(nc.sbuf_tensor("A_xhat", [128, 32 * 1024], BF16))
        A_y = Ec(nc.sbuf_tensor("A_y", [128, 32 * 1024], F16))
        A_st = Ec(nc.sbuf_tensor("A_st", [128, 8192], F32))
        A_sb = Ec(nc.sbuf_tensor("A_sb", [128, 8192], BF16))
        A_yb = A_y.bitcast(BF16)
        A_stH = A_st.bitcast(F16)
        A_stB = A_st.bitcast(BF16)
        A_sbF = A_sb.bitcast(F32)

        ident = Ec(nc.sbuf_tensor("ident", [128, 128], BF16))
        identh = Ec(nc.sbuf_tensor("identh", [128, 128], F16))
        ones = Ec(nc.sbuf_tensor("ones", [128, 512], BF16))
        negones = Ec(nc.sbuf_tensor("negones", [1, 128], BF16))
        bias_cols = Ec(nc.sbuf_tensor("bias_cols", [128, BC_N], F32))
        bias_scl = Ec(nc.sbuf_tensor("bias_scl", [128, BS_N], F32))
        colbuf = Ec(nc.sbuf_tensor("colbuf", [128, 1], BF16))
        rowbuf = Ec(nc.sbuf_tensor("rowbuf", [1, 128], BF16))
        kvrope = Ec(nc.sbuf_tensor("kvrope", [64, 2048], BF16))
        qh_n = [Ec(nc.sbuf_tensor(f"qh_n{i}", [128, 1024], BF16)) for i in range(2)]
        qh_r = [Ec(nc.sbuf_tensor(f"qh_r{i}", [64, 1024], BF16)) for i in range(2)]
        qkv_tmp = [Ec(nc.sbuf_tensor(f"qkv_tmp{i}", [128, 1024], BF16))
                   for i in range(2)]
        ctxa = Ec(nc.sbuf_tensor("ctxa", [128, 1024], BF16))
        bcast = Ec(nc.sbuf_tensor("bcast", [128, 1024], BF16))
        Bg = Ec(nc.sbuf_tensor("Bg", [1, 2048], BF16))
        stats = Ec(nc.sbuf_tensor("stats", [128, 8], F32))
        epsc = Ec(nc.sbuf_tensor("epsc", [128, 2], F32))
        acol = Ec(nc.sbuf_tensor("acol", [128, 32], F32))
        kidx = Ec(nc.sbuf_tensor("kidx", [128, 16], F32))
        qglob = Ec(nc.sbuf_tensor("qglob", [1, T], F32))

        xtv = [A_st.ap()[:, 0:4096], A_st.ap()[:, 4096:8192]]
        xhat_tmp = A_sb.ap()[:, 4096:8192]
        Ebuf = A_yb.ap()[:, 0:4096]
        kvnope = A_yb.ap()[:, 16384:24576]
        maskv = A_yb.ap()[:, 16384:32768]
        qc_sb = A_yb.ap()[:, 0:12288]
        ctxacc = A_y.bitcast(F32).ap()[:, 2048:3072]
        kh_a = [A_yb.ap()[:, 24576 + i * 2048:24576 + (i + 1) * 2048]
                for i in range(2)]
        vh_a = [A_yb.ap()[:, 28672 + i * 2048:28672 + (i + 1) * 2048]
                for i in range(2)]
        kh_b = [A_sb.ap()[:, i * 2048:(i + 1) * 2048] for i in range(2)]
        vhf_b = [A_yb.ap()[:, 4096 + i * 2048:4096 + (i + 1) * 2048]
                 for i in range(2)]
        vh_b = [A_yb.ap()[:, 8192 + i * 2048:8192 + (i + 1) * 2048]
                for i in range(2)]
        kvs_ld = [A_sbF.ap()[:, i * 512:(i + 1) * 512] for i in range(2)]
        kvs_bf = [A_sb.ap()[:, 2048 + i * 512:2048 + (i + 1) * 512]
                  for i in range(2)]
        woT_sb = A_sb.ap()[:, 4096:8192]
        normrow = A_st.ap()[0:1, 0:3072]
        Bf = A_st.ap()[0:1, 3072:4096]
        rsum = A_stB.ap()[0:1, 8192:9216]
        krope_nrm = A_st.ap()[0:1, 5120:7168]
        sqbuf = A_stB.ap()[:, 14336:16384]

        psA = Ec(nc.psum_tensor("psA", [128, 1024], F32))
        psB = Ec(nc.psum_tensor("psB", [128, 1024], F32))
        psC = Ec(nc.psum_tensor("psC", [128, 1024], F32))
        psD = Ec(nc.psum_tensor("psD", [128, 1024], F32))
        bkA, bkB, bkC = Bank(), Bank(), Bank()
        bkDh = [Bank(), Bank()]

        Lw = [Ctr(st, nc, f"Lw{i}") for i in range(2)]
        Lx = [Ctr(st, nc, f"Lx{i}") for i in range(2)]
        Lh = [Ctr(st, nc, f"Lh{i}") for i in range(2)]
        Lonce = Ctr(st, nc, "Lonce")
        Lcc = Ctr(st, nc, "Lcc")
        Lq = Ctr(st, nc, "Lq")
        Lqb = Ctr(st, nc, "Lqb")
        Lsx = [Ctr(st, nc, f"Lsx{i}") for i in range(2)]
        CCa = Ctr(st, nc, "CCa")
        CCb = Ctr(st, nc, "CCb")
        PE = Ctr(st, nc, "PE")
        ACT = Ctr(st, nc, "ACT")
        DVE = Ctr(st, nc, "DVE")
        POOL = Ctr(st, nc, "POOL")
        STO = [Ctr(st, nc, f"STO{i}") for i in range(2)]

        p = Prog()
        p.self_ctr = {"dve": DVE, "act": ACT, "pool": POOL}

        def barrier():
            snap = [(PE, PE.n), (ACT, ACT.n), (DVE, DVE.n), (POOL, POOL.n),
                    (Lw[0], Lw[0].n), (Lw[1], Lw[1].n), (Lx[0], Lx[0].n),
                    (Lx[1], Lx[1].n), (Lh[0], Lh[0].n), (Lh[1], Lh[1].n),
                    (Lonce, Lonce.n), (Lcc, Lcc.n), (Lq, Lq.n),
                    (Lqb, Lqb.n), (Lsx[0], Lsx[0].n), (Lsx[1], Lsx[1].n)]
            for eng in p.ops:
                p.wait(eng, snap)

        def bcol(j, n=128):
            return bias_cols[0:n, j:j + 1]

        def _emit():
            p.op("sp", lambda e: e.dma_start(out=ident[:], in_=identB), inc=Lonce, amt=16)
            p.op("sp", lambda e: e.dma_start(out=identh[:], in_=identH), inc=Lonce, amt=16)
            p.op("sp", lambda e: e.dma_start(out=bias_cols[:], in_=biasD), inc=Lonce, amt=16)
            p.op("sp", lambda e: e.dma_start(out=kidx[:], in_=kidxD), inc=Lonce, amt=16)
            p.op("sp", lambda e: e.dma_start(out=qglob[:], in_=qglobD), inc=Lonce, amt=16)
            ld0 = Lonce.n
            p.op("dve", lambda e: e.memset(ones[:], 1.0), inc=DVE)
            p.op("dve", lambda e: e.memset(epsc[:, 0:1], 1e-5), inc=DVE)
            p.op("dve", lambda e: e.memset(epsc[:, 1:2], 1e-30), inc=DVE)
            p.op("dve", lambda e: e.memset(negones[:], -1.0), inc=DVE)
            p.op("dve", lambda e: e.tensor_scalar(
                bias_scl[:, BS_QUP:BS_QUP + 48], bias_cols[:, BC_QUP:BC_QUP + 48],
                1.0 / math.sqrt(Q_HEAD), None, ALU.mult), waits=[(Lonce, ld0)], inc=DVE)
            p.op("dve", lambda e: e.tensor_scalar(
                bias_scl[:, BS_QKVQ:BS_QKVQ + 32], bias_cols[:, BC_QKV:BC_QKV + 32],
                HD ** -0.5, None, ALU.mult), inc=DVE)

            def fused_pass(src_x, dst_x, psrc_gen, do_ln, skip_store=False):
                xt_dve = [0, 0]
                xst = [None, None]
                for tt in range(8):
                    xb = xtv[tt % 2]
                    stc = STO[tt % 2] if dst_x is yout else Lsx[tt % 2]
                    p.op("sp", lambda e, _xb=xb, _tt=tt: e.dma_start(
                        out=_xb, in_=src_x[_tt * 128:(_tt + 1) * 128, :]),
                        waits=[(DVE, xt_dve[tt % 2]), (ACT, ACT.n),
                               (stc, stc.n)],
                        inc=Lx[tt % 2], amt=16)
                    ldv = (Lx[tt % 2], Lx[tt % 2].n)
                    if psrc_gen is not None:
                        for ft in range(8):
                            ps, pev, bank = psrc_gen(tt, ft)
                            p.op("dve",
                                 lambda e, _xb=xb, _ft=ft, _ps=ps: e.tensor_tensor(
                                     _xb[:, _ft * 512:(_ft + 1) * 512],
                                     _xb[:, _ft * 512:(_ft + 1) * 512],
                                     _ps, ALU.add),
                                 waits=[ldv, (PE, pev)], inc=DVE)
                            bank.rd(DVE)
                    else:
                        p.wait("dve", [ldv])
                    if do_ln:
                        p.op("dve", lambda e, _xb=xb: e.scalar_tensor_tensor(
                            xhat_tmp, _xb, 1.0, _xb, ALU.mult, ALU.mult,
                            accum_out=stats[:, 1:2]), waits=[(PE, PE.n)], inc=DVE)
                        p.op("dve", lambda e, _xb=xb: e.reduce_sum(
                            stats[:, 0:1], _xb, axis=AX.X), inc=DVE)
                        p.op("dve", lambda e: e.tensor_scalar(
                            stats[:, 2:3], stats[:, 0:1], 1.0 / D, None, ALU.mult),
                            inc=DVE)
                        p.op("dve", lambda e: e.tensor_tensor(
                            stats[:, 6:7], stats[:, 2:3], stats[:, 2:3], ALU.mult),
                            inc=DVE)
                        p.op("dve", lambda e: e.scalar_tensor_tensor(
                            stats[:, 5:6], stats[:, 1:2], 1.0 / D, stats[:, 6:7],
                            ALU.mult, ALU.subtract), inc=DVE)
                        dvn = DVE.n
                        p.op("act", lambda e: e.activation(
                            stats[:, 6:7], stats[:, 5:6], AF.Ln,
                            bias=epsc[:, 0:1], scale=1.0),
                            waits=[(DVE, dvn)], inc=ACT)
                        p.op("act", lambda e: e.activation(
                            stats[:, 3:4], stats[:, 6:7], AF.Exp, bias=0.0, scale=-0.5),
                            inc=ACT)
                        acn = ACT.n
                        p.op("dve", lambda e: e.scalar_tensor_tensor(
                            stats[:, 4:5], stats[:, 2:3], -1.0, stats[:, 3:4],
                            ALU.mult, ALU.mult), waits=[(ACT, acn)], inc=DVE)
                        p.op("dve", lambda e, _xb=xb: e.tensor_scalar(
                            xhat_tmp, _xb, stats[:, 3:4], stats[:, 4:5],
                            ALU.mult, ALU.add), inc=DVE)
                        xhd = DVE.n
                        xv = A_xhat.ap().rearrange("p (f c) -> p f c", c=1024)
                        for g in range(8):
                            hf = g % 2
                            bank = bkDh[hf]
                            wv = bank.wr() + [(DVE, xhd)]
                            for j in range(4):
                                fc = g * 4 + j
                                p.op("pe", lambda e, _hf=hf, _j=j, _fc=fc: e.matmul(
                                    psD[:, _hf * 512 + _j * 128:
                                        _hf * 512 + (_j + 1) * 128],
                                    xhat_tmp[:, _fc * 128:(_fc + 1) * 128],
                                    ident[:], start=True, stop=True),
                                    waits=wv, inc=(PE if j == 3 else None))
                                wv = ()
                            pen = PE.n
                            p.op("act", lambda e, _hf=hf, _g=g, _tt=tt: e.copy(
                                xv[:, _g * 4:(_g + 1) * 4, _tt * 128:(_tt + 1) * 128],
                                psD.ap()[:, _hf * 512:(_hf + 1) * 512].rearrange(
                                    "p (a b) -> p a b", a=4)),
                                waits=[(PE, pen)], inc=ACT)
                            bank.rd(ACT)
                    if not skip_store:
                        wvs = [(DVE, DVE.n), (ACT, ACT.n), ldv]
                        p.op("sp", lambda e, _xb=xb, _tt=tt: e.dma_start(
                            out=dst_x[_tt * 128:(_tt + 1) * 128, :], in_=_xb),
                            waits=wvs, inc=stc, amt=16)
                    xt_dve[tt % 2] = DVE.n

            def stream_mm_fm(wblk, n_mt, n_ic, out_cb, fdim=128, rhs=None,
                             mt_order=None, after_mt=None):
                wv_ = wblk.rearrange("(m p) c -> m p c", p=128)
                order = list(mt_order) if mt_order is not None \
                    else list(range(n_mt))
                prev_cast = [0, 0]
                prev_pe = [0, 0]
                loadn = {}

                def emit_load(mt):
                    sl = mt % 2
                    lb = A_st.ap()[:, sl * 4096:sl * 4096 + n_ic * fdim]
                    p.op("act", lambda e, _mt=mt, _lb=lb: e.dma_start(
                        out=_lb, in_=wv_[_mt]),
                        waits=[(DVE, prev_cast[sl])], inc=Lw[sl], amt=16)
                    loadn[mt] = Lw[sl].n

                emit_load(order[0])
                if len(order) > 1:
                    emit_load(order[1])
                for mti, mt in enumerate(order):
                    sl = mt % 2
                    lb = A_st.ap()[:, sl * 4096:sl * 4096 + n_ic * fdim]
                    bb = A_sb.ap()[:, sl * 4096:sl * 4096 + n_ic * fdim]
                    p.op("dve", lambda e, _lb=lb, _bb=bb: e.tensor_copy(_bb, _lb),
                         waits=[(Lw[sl], loadn[mt]), (PE, prev_pe[sl])])
                    prev_cast[sl] = DVE.n
                    cn = DVE.n
                    if mti + 2 < len(order):
                        emit_load(order[mti + 2])
                    ps, bank = (psA, bkA) if sl == 0 else (psB, bkB)
                    wv = bank.wr() + [(DVE, cn)]
                    for nh in range(2):
                        for ic in range(n_ic):
                            src = (rhs if rhs is not None else A_xhat.ap())
                            p.op("pe", lambda e, _ps=ps, _nh=nh, _bb=bb, _ic=ic,
                                 _f=(ic == 0), _l=(ic == n_ic - 1), _fd=fdim,
                                 _s=src: e.matmul(
                                     _ps[0:_fd, _nh * 512:(_nh + 1) * 512],
                                     _bb[:, _ic * _fd:(_ic + 1) * _fd],
                                     _s[:, _ic * 1024 + _nh * 512:
                                        _ic * 1024 + _nh * 512 + 512],
                                     start=_f, stop=_l),
                                waits=wv,
                                inc=(PE if (ic == n_ic - 1 and nh == 1) else None))
                            wv = ()
                    prev_pe[sl] = PE.n
                    out_cb(mt, ps, PE.n, bank)
                    if after_mt is not None:
                        after_mt(mti, mt)

            if upto == 90:
                fused_pass(x0, yout, None, do_ln=False)
                return
            fused_pass(x0, None, None, do_ln=True, skip_store=True)
            barrier()
            if upto == 91:
                fused_pass(x0, yout, None, do_ln=False)
                return

            def qc_out(mt, ps, pev, bank):
                p.op("act", lambda e, _mt=mt, _ps=ps: e.activation(
                    qc_sb[:, _mt * 1024:(_mt + 1) * 1024], _ps[:], AF.Identity,
                    bias=bcol(BC_QC + _mt), scale=1.0), waits=[(PE, pev)], inc=ACT)
                bank.rd(ACT)
            stream_mm_fm(wqcB, 12, 32, qc_out)
            barrier()

            def kvn_out(mt, ps, pev, bank):
                buf = qkv_tmp[mt % 2]
                p.op("act", lambda e, _ps=ps, _b=buf, _mt=mt: e.activation(
                    _b[:], _ps[:], AF.Identity, bias=bcol(BC_KVN + _mt), scale=1.0),
                    waits=[(PE, pev), (Lcc, Lcc.n)], inc=ACT)
                bank.rd(ACT)
                an = ACT.n
                p.op("sp", lambda e, _mt=mt, _b=buf: e.dma_start(
                    out=cc_in_a.ap()[_mt * 128:(_mt + 1) * 128, :], in_=_b[:]),
                    waits=[(ACT, an)], inc=Lcc, amt=16)
            stream_mm_fm(wkvcnB, 4, 32, kvn_out)
            barrier()

            def kvr_out(mt, ps, pev, bank):
                buf = qkv_tmp[0]
                p.op("act", lambda e, _ps=ps, _b=buf: e.activation(
                    _b[0:64, :], _ps[0:64, :], AF.Identity,
                    bias=bcol(BC_KVR, 64), scale=1.0),
                    waits=[(PE, pev), (Lcc, Lcc.n)], inc=ACT)
                bank.rd(ACT)
                an = ACT.n
                p.op("sp", lambda e, _b=buf: e.dma_start(
                    out=cc_in_a.ap()[512:576, :], in_=_b[0:64, :]),
                    waits=[(ACT, an)], inc=Lcc, amt=16)
            stream_mm_fm(wkvcrB, 1, 32, kvr_out, fdim=64)
            barrier()

            p.wait("pool", [(Lcc, Lcc.n)])
            p.op("pool", lambda e: e.collective_compute(
                "AllGather", ALU.bypass,
                replica_groups=[[0, 1], [2, 3], [4, 5], [6, 7]],
                ins=[cc_in_a.ap().opt()], outs=[cc_out_a.ap().opt()]), inc=CCa)
            if upto == 92:
                p.wait("sp", [(CCa, 1)])
                barrier()
                fused_pass(x0, yout, None, do_ln=False)
                return

            def qup_out(mt, ps, pev, bank):
                buf = qkv_tmp[mt % 2]
                p.op("act", lambda e, _ps=ps, _b=buf, _mt=mt: e.activation(
                    _b[:], _ps[:], AF.Identity,
                    bias=bias_scl[:, BS_QUP + _mt:BS_QUP + _mt + 1],
                    scale=1.0 / math.sqrt(Q_HEAD)),
                    waits=[(PE, pev), (Lq, Lq.n), (DVE, DVE.n)], inc=ACT)
                bank.rd(ACT)
                an = ACT.n
                p.op("sp", lambda e, _mt=mt, _b=buf: e.dma_start(
                    out=q_dram[_mt * 128:(_mt + 1) * 128, :], in_=_b[:]),
                    waits=[(ACT, an)], inc=Lq, amt=16)
            stream_mm_fm(wqupB, 48, 12, qup_out, rhs=qc_sb)

            barrier()
            if upto == 93:
                fused_pass(x0, yout, None, do_ln=False)
                return

            cga = cc_out_a.ap()
            p.wait("sp", [(CCa, 1)])
            for c in range(4):
                for r in range(2):
                    p.op("sp", lambda e, _c=c, _r=r: e.dma_start(
                        out=kvnope[:, _c * 2048 + _r * 1024:
                                   _c * 2048 + _r * 1024 + 1024],
                        in_=cga[_r * 576 + _c * 128:_r * 576 + (_c + 1) * 128, :]),
                        inc=Lonce, amt=16)
            for r in range(2):
                p.op("sp", lambda e, _r=r: e.dma_start(
                    out=kvrope[:, _r * 1024:(_r + 1) * 1024],
                    in_=cga[_r * 576 + 512:_r * 576 + 576, :]), inc=Lonce, amt=16)
            p.op("sp", lambda e: e.dma_start(out=A_st[:, 0:4096], in_=woB),
                 inc=Lonce, amt=16)
            kv_loaded = Lonce.n
            _l = Lonce.n
            p.op("pool", lambda e: e.tensor_copy(woT_sb, A_st[:, 0:4096]),
                 waits=[(Lonce, _l)], inc=POOL)
            wo_pool = POOL.n
            p.op("pool", lambda e: e.tensor_tensor(
                sqbuf[0:64, :], kvrope[:], kvrope[:], ALU.mult),
                waits=[(Lonce, kv_loaded)], inc=POOL)
            krn = POOL.n
            for nt in range(4):
                hf = nt % 2
                bank = bkDh[hf]
                wv = bank.wr() + [(POOL, krn)]
                p.op("pe", lambda e, _hf=hf, _nt=nt: e.matmul(
                    psD[0:1, _hf * 512:_hf * 512 + 512], ones[0:64, 0:1],
                    sqbuf[0:64, _nt * 512:(_nt + 1) * 512], start=True, stop=True),
                    waits=wv, inc=PE)
                pev = PE.n
                p.op("act", lambda e, _hf=hf, _nt=nt: e.copy(
                    krope_nrm[0:1, _nt * 512:(_nt + 1) * 512],
                    psD[0:1, _hf * 512:_hf * 512 + 512]),
                    waits=[(PE, pev)], inc=ACT)
                bank.rd(ACT)

            bstate = {"sq_pe": PE.n, "nr_dve": 0, "bf_act": 0}

            def bounds_rows(qn, qr, kbuf, with_rope, ldq,
                            psq=None, bkq=None, psk=None, bkk=None):
                psq = psq if psq is not None else psD
                psk = psk if psk is not None else psD
                bkqs = [bkq, bkq] if bkq is not None else bkDh
                bkks = [bkk, bkk] if bkk is not None else bkDh
                p.op("pool", lambda e, _qn=qn: e.tensor_tensor(
                    sqbuf[:, 0:1024], _qn[:], _qn[:], ALU.mult),
                    waits=[ldq, (PE, bstate["sq_pe"]), (ACT, bstate["bf_act"])],
                    inc=POOL)
                if qr is not None:
                    p.op("pool", lambda e, _qr=qr: e.tensor_tensor(
                        sqbuf[0:64, 1024:2048], _qr[:], _qr[:], ALU.mult), inc=POOL)
                qsq = POOL.n
                for nhf in range(2):
                    bank = bkqs[nhf]
                    wv = bank.wr() + [(POOL, qsq)]
                    p.op("pe", lambda e, _nh=nhf, _ps=psq: e.matmul(
                        _ps[0:1, _nh * 512:_nh * 512 + 512], ones[:, 0:1],
                        sqbuf[:, _nh * 512:(_nh + 1) * 512],
                        start=True, stop=(qr is None)), waits=wv,
                        inc=(None if qr is not None else PE))
                    if qr is not None:
                        p.op("pe", lambda e, _nh=nhf, _ps=psq: e.matmul(
                            _ps[0:1, _nh * 512:_nh * 512 + 512], ones[0:64, 0:1],
                            sqbuf[0:64, 1024 + _nh * 512:1024 + (_nh + 1) * 512],
                            start=False, stop=True), inc=PE)
                    pev = PE.n
                    p.op("dve", lambda e, _nh=nhf, _ps=psq: e.tensor_copy(
                        normrow[0:1, _nh * 512:(_nh + 1) * 512],
                        _ps[0:1, _nh * 512:_nh * 512 + 512]),
                        waits=[(PE, pev), (DVE, bstate["nr_dve"])], inc=DVE)
                    bank.rd(DVE)
                p.op("pool", lambda e, _k=kbuf: e.tensor_tensor(
                    sqbuf[:, 0:2048], _k[:], _k[:], ALU.mult),
                    waits=[(ACT, ACT.n), (DVE, DVE.n)], inc=POOL)
                ksq = POOL.n
                for nt in range(4):
                    hf = nt % 2
                    bank = bkks[hf]
                    wv = bank.wr() + [(POOL, ksq)]
                    p.op("pe", lambda e, _hf=hf, _nt=nt, _ps=psk: e.matmul(
                        _ps[0:1, _hf * 512:_hf * 512 + 512], ones[:, 0:1],
                        sqbuf[:, _nt * 512:(_nt + 1) * 512],
                        start=True, stop=True), waits=wv, inc=PE)
                    pev = PE.n
                    if with_rope:
                        p.op("dve", lambda e, _hf=hf, _nt=nt, _ps=psk:
                             e.tensor_tensor(
                                 normrow[0:1, 1024 + _nt * 512:
                                         1024 + (_nt + 1) * 512],
                                 krope_nrm[0:1, _nt * 512:(_nt + 1) * 512],
                                 _ps[0:1, _hf * 512:_hf * 512 + 512], ALU.add),
                             waits=[(PE, pev)], inc=DVE)
                        bank.rd(DVE)
                    else:
                        p.op("dve", lambda e, _hf=hf, _nt=nt, _ps=psk:
                             e.tensor_copy(
                                 normrow[0:1, 1024 + _nt * 512:
                                         1024 + (_nt + 1) * 512],
                                 _ps[0:1, _hf * 512:_hf * 512 + 512]),
                             waits=[(PE, pev)], inc=DVE)
                        bank.rd(DVE)
                kn = (DVE.n, ACT.n)
                p.op("dve", lambda e: e.reduce_max(
                    stats[0:1, 7:8], normrow[0:1, 1024:3072], axis=AX.X),
                    waits=[(ACT, kn[1]), (DVE, kn[0])], inc=DVE)
                p.op("dve", lambda e: e.tensor_scalar(
                    Bf[0:1, :], normrow[0:1, 0:1024], stats[0:1, 7:8], None,
                    ALU.mult), inc=DVE)
                bstate["nr_dve"] = DVE.n
                bn = DVE.n
                p.op("act", lambda e: e.activation(
                    Bf[0:1, :], Bf[0:1, :], AF.Ln, bias=epsc[0:1, 1:2], scale=1.0),
                    waits=[(DVE, bn)], inc=ACT)
                p.op("act", lambda e: e.activation(
                    Bf[0:1, :], Bf[0:1, :], AF.Exp, bias=0.0, scale=0.5), inc=ACT)
                bstate["sq_pe"] = PE.n
                bstate["bf_act"] = ACT.n
                return ACT.n

            def attn_heads(va, mask_ld=0):
                st_ = {"qh_pe": [0, 0], "aux_pe": [0, 0], "aux_act": [0, 0],
                       "bgr": [0, 0], "ebuf_pe": [0, 0, 0, 0], "kvs_pe": 0,
                       "rowbuf_pe": 0, "bcast_dve": 0, "sums_pe": 0,
                       "recip_dve": 0, "ctx_pe": 0}
                wkv_v2 = wkvupB.rearrange("(s p) c -> s p c", p=128)
                cgb2 = cc_out_b.ap()
                if va == "a":
                    p.op("dve", lambda e: e.memset(ctxacc, 0.0), inc=DVE)
                acc0_dve = DVE.n

                def phase1(h):
                    pr = h % 2
                    qn = qh_n[pr]
                    khb = kh_a[pr] if va == "a" else kh_b[pr]
                    vhb = vh_a[pr] if va == "a" else vh_b[pr]
                    qsrc = q_dram if va == "a" else qb_dram
                    qwv = [(PE, st_["qh_pe"][pr])]
                    if va == "b":
                        qwv.append((Lq, Lq.n))
                    p.op("sp", lambda e, _h=h, _qn=qn, _qs=qsrc: e.dma_start(
                        out=_qn[:], in_=_qs[_h * 128:(_h + 1) * 128, :]),
                        waits=qwv, inc=Lh[pr], amt=16)
                    if va == "a":
                        qr = qh_r[pr]
                        p.op("sp", lambda e, _h=h, _qr=qr: e.dma_start(
                            out=_qr[:],
                            in_=q_dram[4096 + _h * 64:4096 + (_h + 1) * 64, :]),
                            inc=Lh[pr], amt=16)
                        kb, vb = kvs_ld
                        kbb, vbb = kvs_bf
                        p.op("sp", lambda e, _h=h, _kb=kb: e.dma_start(
                            out=_kb, in_=wkv_v2[2 * _h]),
                            waits=[(POOL, POOL.n)], inc=Lh[pr], amt=16)
                        p.op("sp", lambda e, _h=h, _vb=vb: e.dma_start(
                            out=_vb, in_=wkv_v2[2 * _h + 1]), inc=Lh[pr], amt=16)
                        ldall = Lh[pr].n
                        ldq = (Lh[pr], ldall)
                        p.op("pool", lambda e: e.tensor_copy(kbb, kb),
                             waits=[(Lh[pr], ldall), (PE, st_["kvs_pe"])],
                             inc=POOL)
                        p.op("pool", lambda e: e.tensor_copy(vbb, vb), inc=POOL)
                        cn = POOL.n
                        p.op("dve", lambda e, _h=h: e.tensor_copy(
                            colbuf[:], bcol(BC_KVUP + 2 * _h + 1)),
                            waits=[(PE, st_["rowbuf_pe"])], inc=DVE)
                        cbn = DVE.n
                        wv = bkA.wr() + [(DVE, cbn)]
                        p.op("pe", lambda e: e.matmul(
                            psA[0:1, 0:128], colbuf[:], ident[:],
                            start=True, stop=True), waits=wv, inc=PE)
                        rb_pe = PE.n
                        p.op("dve", lambda e: e.tensor_copy(
                            rowbuf[:], psA[0:1, 0:128]),
                            waits=[(PE, rb_pe)], inc=DVE)
                        bkA.rd(DVE)
                        rb_act = DVE.n
                        for half in range(2):
                            ps, bank = (psA, bkA) if half == 0 else (psB, bkB)
                            wv = bank.wr() + [(POOL, cn), (Lonce, kv_loaded)]
                            for nhf in range(2):
                                nt = half * 2 + nhf
                                for ic in range(4):
                                    p.op("pe", lambda e, _ps=ps, _nh=nhf, _nt=nt,
                                         _ic=ic: e.matmul(
                                             _ps[:, _nh * 512:(_nh + 1) * 512],
                                             kvs_bf[0][:, _ic * 128:
                                                       (_ic + 1) * 128],
                                             kvnope[:, _ic * 2048 + _nt * 512:
                                                    _ic * 2048 + _nt * 512 + 512],
                                             start=(_ic == 0), stop=(_ic == 3)),
                                        waits=wv,
                                        inc=(PE if (ic == 3 and nhf == 1)
                                             else None))
                                    wv = ()
                            pev = PE.n
                            p.op("dve", lambda e, _ps=ps, _half=half, _k=khb,
                                 _h=h: e.tensor_scalar(
                                     _k[:, _half * 1024:(_half + 1) * 1024],
                                     _ps[:], bcol(BC_KVUP + 2 * _h), None,
                                     ALU.add),
                                 waits=[(PE, pev)], inc=DVE)
                            bank.rd(DVE)
                        for g in range(4):
                            ps, bank = (psA, bkA) if g % 2 == 0 else (psB, bkB)
                            wv = bank.wr() + [(DVE, rb_act)]
                            for j in range(4):
                                kt = g * 4 + j
                                p.op("pe", lambda e, _ps=ps, _j=j: e.matmul(
                                    _ps[:, _j * 128:(_j + 1) * 128],
                                    ones[0:1, 0:128], rowbuf[0:1, :],
                                    start=True, stop=False), waits=wv)
                                wv = ()
                                for ic in range(4):
                                    p.op("pe", lambda e, _ps=ps, _j=j, _kt=kt,
                                         _ic=ic: e.matmul(
                                             _ps[:, _j * 128:(_j + 1) * 128],
                                             kvnope[:, _ic * 2048 + _kt * 128:
                                                    _ic * 2048 + (_kt + 1) * 128],
                                             kvs_bf[1][:, _ic * 128:
                                                       (_ic + 1) * 128],
                                             start=False, stop=(_ic == 3)),
                                        inc=(PE if (ic == 3 and j == 3)
                                             else None))
                            pev = PE.n
                            p.op("dve", lambda e, _ps=ps, _g=g, _v=vhb:
                                 e.tensor_copy(
                                     _v[:, _g * 512:(_g + 1) * 512],
                                     _ps[:, 0:512]),
                                 waits=[(PE, pev)], inc=DVE)
                            bank.rd(DVE)
                        st_["kvs_pe"] = PE.n
                        st_["rowbuf_pe"] = PE.n
                        ban = bounds_rows(qn, qr, khb, True, ldq,
                                          psq=psA, bkq=bkA, psk=psB, bkk=bkB)
                        p.op("dve", lambda e, _pr=pr: e.tensor_scalar(
                            Bg[0:1, _pr * 1024:(_pr + 1) * 1024], Bf[0:1, :],
                            1.0, None, ALU.mult),
                            waits=[(ACT, ban), (PE, PE.n)], inc=DVE)
                        st_["bgr"][pr] = DVE.n
                    else:
                        vfb = vhf_b[pr]
                        for r in range(2):
                            p.op("sp", lambda e, _h=h, _r=r, _k=khb: e.dma_start(
                                out=_k[:, _r * 1024:(_r + 1) * 1024],
                                in_=cgb2[_r * 8192 + _h * 128:
                                         _r * 8192 + (_h + 1) * 128, :]),
                                waits=[(PE, st_["aux_pe"][pr]), (POOL, POOL.n)],
                                inc=Lh[pr], amt=16)
                            p.op("sp", lambda e, _h=h, _r=r, _v=vfb: e.dma_start(
                                out=_v[:, _r * 1024:(_r + 1) * 1024],
                                in_=cgb2[_r * 8192 + 4096 + _h * 128:
                                         _r * 8192 + 4096 + (_h + 1) * 128, :]),
                                inc=Lh[pr], amt=16)
                        ldall = Lh[pr].n
                        ldq = (Lh[pr], ldall)
                        for g in range(4):
                            ps, bank = (psA, bkA) if g % 2 == 0 else (psB, bkB)
                            wv = bank.wr() + [(Lh[pr], ldall),
                                              (PE, st_["aux_pe"][pr])]
                            for j in range(4):
                                kt = g * 4 + j
                                p.op("pe", lambda e, _ps=ps, _j=j, _kt=kt,
                                     _v=vfb: e.matmul(
                                         _ps[:, _j * 128:(_j + 1) * 128],
                                         _v[:, _kt * 128:(_kt + 1) * 128],
                                         ident[:], start=True, stop=True),
                                    waits=wv, inc=(PE if j == 3 else None))
                                wv = ()
                            pev = PE.n
                            p.op("dve", lambda e, _ps=ps, _g=g, _v=vhb:
                                 e.tensor_copy(
                                     _v[:, _g * 512:(_g + 1) * 512],
                                     _ps[:, 0:512]),
                                 waits=[(PE, pev)], inc=DVE)
                            bank.rd(DVE)
                        ban = bounds_rows(qn, None, khb, False, ldq,
                                          psq=psA, bkq=bkA, psk=psB, bkk=bkB)
                        sl_h = 2.0 ** (-0.25 * (h + 1))
                        p.op("dve", lambda e, _s=sl_h: e.scalar_tensor_tensor(
                            Bf[0:1, :], qglob[0:1, :], _s, Bf[0:1, :],
                            ALU.mult, ALU.add),
                            waits=[(ACT, ban), (PE, PE.n)], inc=DVE)
                        p.op("dve", lambda e, _pr=pr: e.tensor_scalar(
                            Bg[0:1, _pr * 1024:(_pr + 1) * 1024], Bf[0:1, :],
                            1.0, None, ALU.mult), inc=DVE)
                        p.op("dve", lambda e, _s=sl_h, _pr=pr: e.tensor_scalar(
                            acol[:, _pr * 16:(_pr + 1) * 16], kidx[:],
                            _s, None, ALU.mult),
                            waits=[(ACT, st_["aux_act"][pr])], inc=DVE)
                        st_["bgr"][pr] = DVE.n
                    st_["aux_act"][pr] = ACT.n

                def phase2(h, inject=None):
                    pr = h % 2
                    qn = qh_n[pr]
                    qr = qh_r[pr] if va == "a" else None
                    khb = kh_a[pr] if va == "a" else kh_b[pr]
                    vhb = vh_a[pr] if va == "a" else vh_b[pr]
                    aux = st_["aux_act"][pr]
                    expk_of = {}

                    def consume(kt):
                        slot = kt % 4
                        expk = expk_of.pop(kt)
                        wv1 = (bkDh[0].wr() + bkDh[1].wr()) if kt == 0 else []
                        wv2 = bkC.wr() if kt == 0 else []
                        for q5 in range(2):
                            p.op("pe", lambda e, _sl=slot, _q5=q5,
                                 _kt=kt: e.matmul(
                                     psD[0:1, _q5 * 512:_q5 * 512 + 512],
                                     ones[:, 0:1],
                                     Ebuf[:, _sl * 1024 + _q5 * 512:
                                          _sl * 1024 + _q5 * 512 + 512],
                                     start=(_kt == 0), stop=(_kt == 15)),
                                waits=(wv1 + [(ACT, expk)]),
                                inc=(PE if (kt == 15 and q5 == 1) else None))
                            wv1 = []
                        for q5 in range(2):
                            p.op("pe", lambda e, _sl=slot, _q5=q5, _kt=kt,
                                 _v=vhb: e.matmul(
                                     psC[:, _q5 * 512:(_q5 + 1) * 512],
                                     _v[:, _kt * 128:(_kt + 1) * 128],
                                     Ebuf[:, _sl * 1024 + _q5 * 512:
                                          _sl * 1024 + _q5 * 512 + 512],
                                     start=(_kt == 0), stop=(_kt == 15)),
                                waits=wv2, inc=(PE if q5 == 1 else None))
                            wv2 = []
                        st_["ebuf_pe"][slot] = PE.n

                    for kt in range(16):
                        if kt == 4 and inject is not None:
                            inject()
                        slot = kt % 4
                        ps, bank = (psA, bkA) if kt % 2 == 0 else (psB, bkB)
                        wv = bank.wr() + [(DVE, st_["bgr"][pr]), (ACT, aux)]
                        if va == "b":
                            wv = wv + [(Lonce, mask_ld)]
                        for q5 in range(2):
                            p.op("pe", lambda e, _ps=ps, _kt=kt, _q5=q5, _qn=qn,
                                 _k=khb: e.matmul(
                                     _ps[:, _q5 * 512:(_q5 + 1) * 512],
                                     _k[:, _kt * 128:(_kt + 1) * 128],
                                     _qn[:, _q5 * 512:(_q5 + 1) * 512],
                                     start=True, stop=False), waits=wv)
                            wv = ()
                        if va == "a":
                            for q5 in range(2):
                                p.op("pe", lambda e, _ps=ps, _kt=kt, _q5=q5,
                                     _qr=qr: e.matmul(
                                         _ps[:, _q5 * 512:(_q5 + 1) * 512],
                                         kvrope[:, _kt * 128:(_kt + 1) * 128],
                                         _qr[:, _q5 * 512:(_q5 + 1) * 512],
                                         start=False, stop=False))
                        rank_stop = (va == "a")
                        for q5 in range(2):
                            p.op("pe", lambda e, _ps=ps, _q5=q5, _pr=pr,
                                 _st=rank_stop: e.matmul(
                                     _ps[:, _q5 * 512:(_q5 + 1) * 512],
                                     negones[0:1, 0:128],
                                     Bg[0:1, _pr * 1024 + _q5 * 512:
                                        _pr * 1024 + _q5 * 512 + 512],
                                     start=False, stop=_st),
                                inc=(PE if (rank_stop and q5 == 1) else None))
                        if va == "b":
                            for q5 in range(2):
                                p.op("pe", lambda e, _ps=ps, _kt=kt,
                                     _q5=q5: e.matmul(
                                         _ps[:, _q5 * 512:(_q5 + 1) * 512],
                                         ident[:],
                                         maskv[:, _q5 * 8192 + _kt * 512:
                                               _q5 * 8192 + (_kt + 1) * 512],
                                         start=False, stop=True),
                                    inc=(PE if q5 == 1 else None))
                        pev = PE.n
                        bias = 0.0 if va == "a" else acol[:, pr * 16 + kt:
                                                          pr * 16 + kt + 1]
                        p.op("act", lambda e, _ps=ps, _sl=slot, _bi=bias:
                             e.activation(
                                 Ebuf[:, _sl * 1024:(_sl + 1) * 1024], _ps[:],
                                 AF.Exp, bias=_bi, scale=1.0),
                             waits=[(PE, pev), (PE, st_["ebuf_pe"][slot])],
                             inc=ACT)
                        bank.rd(ACT)
                        expk_of[kt] = ACT.n
                        if kt > 0:
                            consume(kt - 1)
                    consume(15)
                    st_["sums_pe"] = PE.n
                    st_["ctx_pe"] = PE.n

                    def _recipu(e):
                        with nc.allow_low_precision(
                                reason="bf16 softmax normalizer"):
                            return e.reciprocal(rsum[0:1, 0:1024],
                                                psD[0:1, 0:1024])
                    p.op("dve", _recipu, waits=[(PE, st_["sums_pe"])], inc=DVE)
                    bkDh[0].rd(DVE)
                    bkDh[1].rd(DVE)
                    st_["recip_dve"] = DVE.n
                    st_["qh_pe"][pr] = PE.n
                    st_["aux_pe"][pr] = PE.n

                def phase3(h):
                    wv = bkDh[0].wr() + bkDh[1].wr() + [(DVE, st_["recip_dve"])]
                    for q5 in range(2):
                        p.op("pe", lambda e, _q5=q5: e.matmul(
                            psD[:, _q5 * 512:(_q5 + 1) * 512], ones[0:1, 0:128],
                            rsum[0:1, _q5 * 512:(_q5 + 1) * 512],
                            start=True, stop=True), waits=wv,
                            inc=(PE if q5 == 1 else None))
                        wv = []
                    bb_pe = PE.n
                    p.op("act", lambda e: e.copy(bcast[:], psD[:]),
                         waits=[(PE, bb_pe), (DVE, st_["bcast_dve"])], inc=ACT)
                    bkDh[0].rd(ACT)
                    bkDh[1].rd(ACT)
                    bc_act = ACT.n
                    if va == "b":
                        p.op("dve", lambda e, _h=h: e.tensor_tensor(
                            A_xhat[:, _h * 1024:(_h + 1) * 1024], psC[:],
                            bcast[:], ALU.mult),
                            waits=[(ACT, bc_act), (PE, st_["ctx_pe"])], inc=DVE)
                        bkC.rd(DVE)
                    else:
                        p.op("dve", lambda e: e.tensor_tensor(
                            bcast[:], psC[:], bcast[:], ALU.mult),
                            waits=[(ACT, bc_act), (PE, st_["ctx_pe"]),
                                   (DVE, acc0_dve)], inc=DVE)
                        bkC.rd(DVE)
                        p.op("dve", lambda e: e.tensor_tensor(
                            ctxacc, ctxacc, bcast[:], ALU.add), inc=DVE)
                    st_["bcast_dve"] = DVE.n

                phase1(0)
                for h in range(NH):
                    phase2(h)
                    if h + 1 < NH:
                        phase1(h + 1)
                    phase3(h)
                return DVE.n

            acc_dve = attn_heads("a")
            p.op("act", lambda e: e.copy(ctxa[:], ctxacc),
                 waits=[(DVE, acc_dve)], inc=ACT)
            ctx_act = ACT.n
            if upto == 94:
                barrier()
                fused_pass(x0, yout, None, do_ln=False)
                return

            prev_act_oa = [0, 0]
            for mt in range(32):
                ps, bank = (psA, bkA) if mt % 2 == 0 else (psB, bkB)
                wv = bank.wr() + [(ACT, ctx_act)]
                for nh in range(2):
                    p.op("pe", lambda e, _ps=ps, _nh=nh, _mt=mt: e.matmul(
                        _ps[:, _nh * 512:(_nh + 1) * 512],
                        woT_sb[:, _mt * 128:(_mt + 1) * 128],
                        ctxa[:, _nh * 512:(_nh + 1) * 512],
                        start=True, stop=True),
                        waits=wv, inc=(PE if nh == 1 else None))
                    wv = ()
                pev = PE.n
                p.op("act", lambda e, _ps=ps, _mt=mt: e.activation(
                    A_y[:, _mt * 1024:(_mt + 1) * 1024], _ps[:], AF.Identity,
                    bias=bcol(BC_BO + _mt), scale=1.0), waits=[(PE, pev)], inc=ACT)
                bank.rd(ACT)
            yd_a = ACT.n
            if upto == 95:
                barrier()
                fused_pass(x0, yout, None, do_ln=False)
                return

            def mk_psrc(yd):
                def psrc(tt, ft):
                    hf = ft % 2
                    bank = bkDh[hf]
                    wv = bank.wr() + [(ACT, yd)]
                    for j in range(4):
                        oc = ft * 4 + j
                        p.op("pe", lambda e, _hf=hf, _j=j, _oc=oc,
                             _tt=tt: e.matmul(
                                 psD[:, _hf * 512 + _j * 128:
                                     _hf * 512 + (_j + 1) * 128],
                                 A_y[:, _oc * 1024 + _tt * 128:
                                     _oc * 1024 + (_tt + 1) * 128],
                                 identh[:], start=True, stop=True),
                            waits=wv, inc=(PE if j == 3 else None))
                        wv = ()
                    return (psD[:, hf * 512:(hf + 1) * 512], PE.n, bank)
                return psrc

            barrier()
            if upto == 1:
                fused_pass(x0, yout, mk_psrc(yd_a), do_ln=False)
            else:
                fused_pass(x0, xA, mk_psrc(yd_a), do_ln=True)

            def ffn_phase(src_x, dst_x, final):
                barrier()
                wff = wffnB.rearrange("(m p) c -> m p c", p=128)
                srp = srepB.rearrange("(m p) c -> m p c", p=128)
                prev_pool = [0, 0]
                prev_pe = [0, 0]
                for ot in range(32):
                    sl = ot % 2
                    lw = A_stH.ap()[:, sl * 8192:sl * 8192 + 4096]
                    ls = A_stH.ap()[:, sl * 8192 + 4096:sl * 8192 + 8192]
                    bb = A_sb.ap()[:, sl * 4096:(sl + 1) * 4096]
                    p.op("sp", lambda e, _ot=ot, _lw=lw: e.dma_start(
                        out=_lw, in_=wff[_ot]), waits=[(POOL, prev_pool[sl])],
                        inc=Lw[sl], amt=16)
                    p.op("sp", lambda e, _ot=ot, _ls=ls: e.dma_start(
                        out=_ls, in_=srp[_ot]), inc=Lw[sl], amt=16)
                    ldn = Lw[sl].n
                    p.op("pool", lambda e, _lw=lw, _ls=ls, _bb=bb: e.tensor_tensor(
                        _bb, _lw, _ls, ALU.mult),
                        waits=[(Lw[sl], ldn), (PE, prev_pe[sl])], inc=POOL)
                    prev_pool[sl] = POOL.n
                    cn = POOL.n
                    ps, bank = (psA, bkA) if sl == 0 else (psB, bkB)
                    wv = bank.wr() + [(POOL, cn)]
                    for nh in range(2):
                        for ic in range(32):
                            p.op("pe", lambda e, _ps=ps, _nh=nh, _bb=bb,
                                 _ic=ic: e.matmul(
                                     _ps[:, _nh * 512:(_nh + 1) * 512],
                                     _bb[:, _ic * 128:(_ic + 1) * 128],
                                     A_xhat[:, _ic * 1024 + _nh * 512:
                                            _ic * 1024 + _nh * 512 + 512],
                                     start=(_ic == 0), stop=(_ic == 31)),
                                waits=wv,
                                inc=(PE if (ic == 31 and nh == 1) else None))
                            wv = ()
                    prev_pe[sl] = PE.n
                    pev = PE.n
                    p.op("act", lambda e, _ps=ps, _ot=ot: e.copy(
                        A_y[:, _ot * 1024:(_ot + 1) * 1024], _ps[:]),
                        waits=[(PE, pev)], inc=ACT)
                    bank.rd(ACT)
                yd = ACT.n
                barrier()
                fused_pass(src_x, dst_x, mk_psrc(yd), do_ln=not final)

            if upto >= 2:
                ffn_phase(xA, yout if upto == 2 else xB, final=(upto == 2))

            def attn_b_phase(src_x, dst_x, final):
                barrier()

                def qkv_out(mt, ps, pev, bank):
                    buf = qkv_tmp[mt % 2]
                    if mt < 32:
                        bias = bias_scl[:, BS_QKVQ + mt:BS_QKVQ + mt + 1]
                        scale = HD ** -0.5
                        dst = qb_dram[mt * 128:(mt + 1) * 128, :]
                        sem = Lq
                    else:
                        bias = bcol(BC_QKV + mt)
                        scale = 1.0
                        sem = Lqb
                        if mt < 64:
                            dst = cc_in_b.ap()[(mt - 32) * 128:(mt - 31) * 128, :]
                        else:
                            dst = cc_in_b.ap()[4096 + (mt - 64) * 128:
                                               4096 + (mt - 63) * 128, :]
                    p.op("act", lambda e, _ps=ps, _b=buf, _bi=bias, _s=scale:
                         e.activation(_b[:], _ps[:], AF.Identity, bias=_bi, scale=_s),
                         waits=[(PE, pev), (Lqb, Lqb.n), (Lq, Lq.n),
                                (DVE, DVE.n)], inc=ACT)
                    bank.rd(ACT)
                    an = ACT.n
                    p.op("sp", lambda e, _b=buf, _d=dst: e.dma_start(
                        out=_d, in_=_b[:]), waits=[(ACT, an)], inc=sem, amt=16)
                def qkv_after(mti, mt):
                    if mti == 63:
                        p.wait("pool", [(Lqb, Lqb.n)])
                        p.op("pool", lambda e: e.collective_compute(
                            "AllGather", ALU.bypass,
                            replica_groups=[[0, 1], [2, 3], [4, 5], [6, 7]],
                            ins=[cc_in_b.ap().opt()],
                            outs=[cc_out_b.ap().opt()]), inc=CCb)
                stream_mm_fm(wqkvB, 96, 32, qkv_out,
                             mt_order=list(range(32, 96)) + list(range(32)),
                             after_mt=qkv_after)
                p.wait("sp", [(ACT, ACT.n), (DVE, DVE.n), (PE, PE.n)])
                p.op("sp", lambda e: e.dma_start(out=maskv, in_=maskD),
                     inc=Lonce, amt=16)
                mask_ld = Lonce.n
                p.wait("sp", [(CCb, 1)])

                attn_heads("b", mask_ld=mask_ld)
                barrier()

                wvo = woutB.rearrange("(m p) c -> m p c", p=128)
                prev_cast = [0, 0]
                prev_pe = [0, 0]
                loadn = {}

                def emit_load_o(mt):
                    sl = mt % 2
                    lb = A_st.ap()[:, sl * 4096:(sl + 1) * 4096]
                    p.op("act", lambda e, _mt=mt, _lb=lb: e.dma_start(
                        out=_lb, in_=wvo[_mt]),
                        waits=[(DVE, prev_cast[sl])], inc=Lw[sl], amt=16)
                    loadn[mt] = Lw[sl].n

                emit_load_o(0)
                emit_load_o(1)
                for mt in range(32):
                    sl = mt % 2
                    lb = A_st.ap()[:, sl * 4096:(sl + 1) * 4096]
                    bb = A_sb.ap()[:, sl * 4096:(sl + 1) * 4096]
                    p.op("dve", lambda e, _lb=lb, _bb=bb: e.tensor_copy(_bb, _lb),
                         waits=[(Lw[sl], loadn[mt]), (PE, prev_pe[sl])])
                    prev_cast[sl] = DVE.n
                    cn = DVE.n
                    if mt + 2 < 32:
                        emit_load_o(mt + 2)
                    ps, bank = (psA, bkA) if sl == 0 else (psB, bkB)
                    wv = bank.wr() + [(DVE, cn)]
                    for nh in range(2):
                        for ic in range(32):
                            p.op("pe", lambda e, _ps=ps, _nh=nh, _bb=bb,
                                 _ic=ic: e.matmul(
                                     _ps[:, _nh * 512:(_nh + 1) * 512],
                                     _bb[:, _ic * 128:(_ic + 1) * 128],
                                     A_xhat[:, _ic * 1024 + _nh * 512:
                                            _ic * 1024 + _nh * 512 + 512],
                                     start=(_ic == 0), stop=(_ic == 31)),
                                waits=wv,
                                inc=(PE if (ic == 31 and nh == 1) else None))
                            wv = ()
                    prev_pe[sl] = PE.n
                    pev = PE.n
                    p.op("act", lambda e, _ps=ps, _mt=mt: e.activation(
                        A_y[:, _mt * 1024:(_mt + 1) * 1024], _ps[:], AF.Identity,
                        bias=bcol(BC_OUT + _mt), scale=1.0),
                        waits=[(PE, pev)], inc=ACT)
                    bank.rd(ACT)
                yd = ACT.n
                barrier()
                fused_pass(src_x, dst_x, mk_psrc(yd), do_ln=not final)

            if upto >= 3:
                attn_b_phase(xB, yout if upto == 3 else xA, final=(upto == 3))
            if upto >= 4:
                ffn_phase(xA, yout, final=True)

            barrier()
            p.wait("sp", [(STO[0], STO[0].n), (STO[1], STO[1].n)])

        _emit()


        with nc.Block() as block:
            @block.tensor
            def _(e):
                for fn in p.ops["pe"]:
                    fn(e)

            @block.scalar
            def _(e):
                for fn in p.ops["act"]:
                    fn(e)

            @block.vector
            def _(e):
                for fn in p.ops["dve"]:
                    fn(e)

            @block.gpsimd
            def _(e):
                for fn in p.ops["pool"]:
                    fn(e)

            @block.sync
            def _(e):
                for fn in p.ops["sp"]:
                    fn(e)

    return nc



def _prep_inputs(d):
    out = {}
    out["wqcB"] = _blocked(np.ascontiguousarray(d["L0_attn_a_Wq_c"].T), 32, 12) \
        .reshape(12 * 128, 32 * 128)
    perm = np.concatenate(
        [np.arange(h * 192, h * 192 + 128) for h in range(NH)]
        + [np.arange(h * 192 + 128, h * 192 + 192) for h in range(NH)])
    out["wqupB"] = _blocked(
        np.ascontiguousarray(d["L0_attn_a_Wq_up"][perm].T), 12, 48) \
        .reshape(48 * 128, 12 * 128)
    wkvc = np.asarray(d["L0_attn_a_Wkv_c"])
    out["wkvcnB"] = _blocked(np.ascontiguousarray(wkvc[:512].T), 32, 4) \
        .reshape(4 * 128, 32 * 128)
    out["wkvcrB"] = _blocked(np.ascontiguousarray(wkvc[512:576].T), 32, 1, f=64) \
        .reshape(128, 32 * 64)
    kvupT = np.ascontiguousarray(np.asarray(d["L0_attn_a_Wkv_up"]).T)
    sl = np.zeros((64, 128, 512), dtype=np.float32)
    for h in range(NH):
        for i, (a, b) in enumerate([(h * 256, h * 256 + 128),
                                    (h * 256 + 128, h * 256 + 256)]):
            sl[2 * h + i] = kvupT[:, a:b].reshape(4, 128, 128) \
                .transpose(1, 0, 2).reshape(128, 512)
    out["wkvupB"] = np.ascontiguousarray(sl.reshape(64 * 128, 512))
    out["woB"] = np.ascontiguousarray(np.asarray(d["L0_attn_a_Wo"]).T)
    out["wqkvB"] = _blocked(
        np.ascontiguousarray(np.asarray(d["L0_attn_b_w_qkv"])), 32, 96) \
        .reshape(96 * 128, 32 * 128)
    out["woutB"] = _blocked(
        np.ascontiguousarray(np.asarray(d["L0_attn_b_w_out"])), 32, 32) \
        .reshape(32 * 128, 32 * 128)
    out["wffnB"] = _blocked(
        np.ascontiguousarray(np.asarray(d["L0_ffn_weight"]).T), 32, 32) \
        .reshape(32 * 128, 32 * 128)
    sc = np.asarray(d["L0_ffn_scales"])
    s2 = np.ascontiguousarray(sc.reshape(32, 128, 32).transpose(0, 2, 1))
    out["srepB"] = np.ascontiguousarray(
        np.broadcast_to(s2[:, None, :, :], (32, 128, 32, 128))) \
        .reshape(32 * 128, 32 * 128)

    bq_up_perm = np.asarray(d["L0_attn_a_bq_up"])[perm]
    bkvup_sl = np.zeros(8192, dtype=np.float32)
    for h in range(NH):
        bkvup_sl[2 * h * 128:(2 * h + 1) * 128] = \
            d["L0_attn_a_bkv_up"][h * 256:h * 256 + 128]
        bkvup_sl[(2 * h + 1) * 128:(2 * h + 2) * 128] = \
            d["L0_attn_a_bkv_up"][h * 256 + 128:h * 256 + 256]
    cols = np.zeros((128, BC_N), dtype=np.float32)

    def put(base, vec):
        v = np.asarray(vec, dtype=np.float32).reshape(-1)
        nc_ = len(v) // 128 if len(v) % 128 == 0 else None
        if nc_ is None:
            assert len(v) == 64
            cols[0:64, base] = v
        else:
            cols[:, base:base + nc_] = v.reshape(nc_, 128).T
    put(BC_QC, d["L0_attn_a_bq_c"])
    put(BC_QUP, bq_up_perm)
    put(BC_KVN, d["L0_attn_a_bkv_c"][:512])
    put(BC_KVR, d["L0_attn_a_bkv_c"][512:576])
    put(BC_KVUP, bkvup_sl)
    put(BC_BO, d["L0_attn_a_bo"])
    put(BC_QKV, d["L0_attn_b_b_qkv"])
    put(BC_OUT, d["L0_attn_b_b_out"])
    out["biasD"] = np.ascontiguousarray(cols)

    out["identB"] = _bf(np.eye(128, dtype=np.float32))
    out["identH"] = np.ascontiguousarray(np.eye(128, dtype=np.float16))
    out["kidxD"] = np.ascontiguousarray(
        (np.arange(128)[:, None] + 128.0 * np.arange(16)[None, :])
        .astype(np.float32))
    return out


def kernel(**inputs):
    d = inputs
    prep = _prep_inputs(d)
    x = np.ascontiguousarray(np.asarray(d["L0_attn_a_hidden_size"],
                                        dtype=np.float32))
    xsh = x.reshape(B_, 2, T, D)

    nc = build_nc(_UPTO)
    in_maps = []
    kkr = np.arange(128)
    qqr = np.arange(512)
    for c in range(8):
        b, half = c // 2, c % 2
        m = dict(prep)
        m["x0"] = np.ascontiguousarray(xsh[b, half])
        off = half * T
        mask = np.zeros((2, 16, 128, 512), dtype=np.float32)
        for q5 in range(2):
            for kt in range(16):
                kg = kt * 128 + kkr[:, None]
                qg = off + q5 * 512 + qqr[None, :]
                mask[q5, kt] = np.where(kg <= qg, 0.0, MASKVAL)
        m["maskD"] = _bf(np.ascontiguousarray(
            mask.transpose(2, 0, 1, 3)).reshape(128, 16384))
        m["qglobD"] = np.ascontiguousarray(
            (off + np.arange(T, dtype=np.float32))[None, :])
        in_maps.append(m)

    res = run_bass_kernel_spmd(nc, in_maps, core_ids=list(range(8)))
    outs = np.stack([np.stack([res.results[2 * b + h]["yout"]
                               for h in range(2)]) for b in range(B_)])
    return np.ascontiguousarray(outs.reshape(B_, S, D)).astype(np.float32)
